# version 10
# baseline (speedup 1.0000x reference)
from contextlib import ExitStack

import numpy as np
import concourse.bass as bass
import concourse.mybir as mybir
from concourse.bass_utils import run_bass_kernel_spmd

F32 = mybir.dt.float32
BF16 = mybir.dt.bfloat16
AF = mybir.ActivationFunctionType

D = 4096
T = 4096
BATCH = 4
H = 32
DT = D // 128
TOWN = 2048
CH = 512
NCH = TOWN // CH
M2 = 8192
EXTS = [[4, 8, 28, 32], [12, 16, 20, 24]]
MW = 4
Q0S = [[0, 512, 3072, 3584], [1024, 1536, 2048, 2560]]
SCALE = float(np.float32(1.0) / np.float32(np.sqrt(np.float32(128.0))))
ESHIFT = -12.0
EPS = 1e-5


PAR_OVERRIDE = None


def _par_iter(eng, idx):
    if PAR_OVERRIDE is not None:
        return [PAR_OVERRIDE]
    return eng.Switch(idx, 2)


class Sem:
    def __init__(self, nc, es, name):
        self.h = es.enter_context(nc.semaphore(name))
        self.n = 0


def build_program():
    nc = bass.Bass("TRN2", target_bir_lowering=False)

    xT = nc.dram_tensor("xT", [D, T], F32, kind="ExternalInput")
    xTown = nc.dram_tensor("xTown", [D, TOWN], F32, kind="ExternalInput")
    wqT = nc.dram_tensor("wqT", [D, D], F32, kind="ExternalInput")
    wkvT = nc.dram_tensor("wkvT", [D, 256], F32, kind="ExternalInput")
    woT = nc.dram_tensor("woT", [D, D], F32, kind="ExternalInput")
    w12T = nc.dram_tensor("w12T", [D, M2], F32, kind="ExternalInput")
    w3T = nc.dram_tensor("w3T", [D, D], F32, kind="ExternalInput")
    bqd = nc.dram_tensor("bqw", [128, 32], F32, kind="ExternalInput")
    bod = nc.dram_tensor("bow", [128, 32], F32, kind="ExternalInput")
    b12d = nc.dram_tensor("b12w", [128, 64], F32, kind="ExternalInput")
    b3d = nc.dram_tensor("b3w", [128, 32], F32, kind="ExternalInput")
    bkvd = nc.dram_tensor("bkvw", [128, 2], F32, kind="ExternalInput")
    masksd = nc.dram_tensor("masks", [NCH, 128, MW * CH], F32, kind="ExternalInput")
    identd = nc.dram_tensor("identw", [128, 128], F32, kind="ExternalInput")

    qspc = nc.dram_tensor("qspc", [NCH, H, 128, CH], BF16, kind="Internal")
    p1sp = nc.dram_tensor("p1sp", [D, TOWN], F32, kind="Internal")
    x4sp = nc.dram_tensor("x4sp", [D, TOWN], BF16, kind="Internal")
    p2sp = nc.dram_tensor("p2sp", [D, TOWN], F32, kind="Internal")
    outT = nc.dram_tensor("outT", [D, TOWN], F32, kind="ExternalOutput")

    with ExitStack() as es:
        ones_mat = es.enter_context(nc.sbuf_tensor("ones_mat", [128, 128], BF16))
        ones_f32 = es.enter_context(nc.sbuf_tensor("ones_f32", [128, 128], F32))
        eshift = es.enter_context(nc.sbuf_tensor("eshift", [128, 1], F32))
        ident = es.enter_context(nc.sbuf_tensor("ident", [128, 128], BF16))
        bq = es.enter_context(nc.sbuf_tensor("bq", [128, 32], F32))
        bo = es.enter_context(nc.sbuf_tensor("bo", [128, 32], F32))
        b12 = es.enter_context(nc.sbuf_tensor("b12", [128, 64], F32))
        b3 = es.enter_context(nc.sbuf_tensor("b3", [128, 32], F32))
        bkv = es.enter_context(nc.sbuf_tensor("bkv", [128, 2], F32))

        SG = [Sem(nc, es, "s_ga"), Sem(nc, es, "s_gb")]
        SS = [Sem(nc, es, "s_sa"), Sem(nc, es, "s_sb")]
        SO = [Sem(nc, es, "s_oa"), Sem(nc, es, "s_ob")]
        SQ = [Sem(nc, es, "s_qa"), Sem(nc, es, "s_qb")]
        SQR = [Sem(nc, es, "s_qr0"), Sem(nc, es, "s_qr1"), Sem(nc, es, "s_qr2")]
        SMg = Sem(nc, es, "s_mg")
        SMs = Sem(nc, es, "s_ms")
        s_t1 = Sem(nc, es, "s_t1")
        s_t2 = Sem(nc, es, "s_t2")
        s_t5 = Sem(nc, es, "s_t5")
        s_a1 = Sem(nc, es, "s_a1")
        s_g1 = Sem(nc, es, "s_g1")
        s_v1 = Sem(nc, es, "s_v1")
        s_v2 = Sem(nc, es, "s_v2")
        s_v3 = Sem(nc, es, "s_v3")

        Mg0, Ms0 = SMg.n, SMs.n
        with nc.Block("consts") as blk:
            @blk.gpsimd
            def _(g):
                g.dma_start(ident[:], identd[:, :]).then_inc(SMg.h, 16)
                g.memset(ones_mat[:], 1.0)
                g.memset(ones_f32[:], 1.0)
                g.memset(eshift[:], ESHIFT)

            @blk.scalar
            def _(sc):
                for dst, src in ((bq, bqd), (bo, bod), (b12, b12d), (b3, b3d),
                                 (bkv, bkvd)):
                    sc.dma_start(dst[:], src[:, :]).then_inc(SMs.h, 16)
                sc.wait_ge(SMs.h, Ms0 + 80)

            @blk.sync
            def _(sy):
                sy.wait_ge(SMg.h, Mg0 + 16)
        SMg.n += 16
        SMs.n += 80

        es_p1 = ExitStack()
        P1 = [es_p1.enter_context(nc.sbuf_tensor(f"p1st{i}", [128, TOWN], F32))
              for i in range(2)]
        es_kv = ExitStack()
        KT = es_kv.enter_context(nc.sbuf_tensor("KT", [128, T], BF16))
        Vt = es_kv.enter_context(nc.sbuf_tensor("Vt", [128, DT, 128], BF16))
        es_xo = ExitStack()
        xo = es_xo.enter_context(nc.sbuf_tensor("xo", [128, DT, TOWN], BF16))

        KCH = 128
        NKC = T // KCH
        ph0 = ExitStack()
        VTfm = ph0.enter_context(nc.sbuf_tensor("VTfm", [128, T], BF16))
        wkv = ph0.enter_context(nc.sbuf_tensor("wkv", [128, DT, 256], BF16))
        xtmp = [ph0.enter_context(nc.sbuf_tensor(f"xtmp{i}", [128, DT, KCH], BF16))
                for i in range(2)]
        pp0 = ExitStack()
        kp = pp0.enter_context(nc.psum_tensor("kp", [128, KCH], F32))
        vp = pp0.enter_context(nc.psum_tensor("vp", [128, KCH], F32))
        tp = [pp0.enter_context(nc.psum_tensor(f"tp{i}", [128, 128], BF16))
              for i in range(2)]
        ph0.close()
        pp0.close()
        ph1 = ExitStack()
        wqs = [ph1.enter_context(nc.sbuf_tensor(f"wqs{i}", [128, DT, 128], BF16))
               for i in range(4)]
        qst = [ph1.enter_context(nc.sbuf_tensor(f"qst{i}", [128, TOWN], BF16))
               for i in range(2)]
        pp1 = ExitStack()
        qp = [pp1.enter_context(nc.psum_tensor(f"qp{i}", [128, TOWN], F32))
              for i in range(2)]
        ph1.close()
        pp1.close()

        M0 = SMg.n
        GB = [SG[0].n, SG[1].n]
        XO0 = SS[0].n
        OB = [SO[0].n, SO[1].n]
        WQS = [SG[0], SG[1], SQ[0], SQ[1]]
        WB = [SG[0].n + 16 * (NKC // 2), SG[1].n + 16 * (NKC // 2),
              SQ[0].n, SQ[1].n]
        T1_0, T2_0, A1_0, V1_0 = s_t1.n, s_t2.n, s_a1.n, s_v1.n
        TQ_0 = T1_0 + NKC
        AQ_0 = A1_0 + 2 * NKC

        with nc.Block("b1") as blk:
            @blk.gpsimd
            def _(g):
                g.dma_start(wkv[:], wkvT[:, :].rearrange("(a p) o -> p a o", p=128)
                            ).then_inc(SMg.h, 16)
                for i in range(NKC):
                    if i >= 2:
                        g.wait_ge(s_t1.h, T1_0 + i - 1)
                    g.dma_start(
                        xtmp[i % 2][:],
                        xT[:, i * KCH:(i + 1) * KCH].rearrange(
                            "(a p) t -> p a t", p=128)
                    ).then_inc(SG[i % 2].h, 16)
                    if i % 8 == 7:
                        j = i // 8
                        g.dma_start(
                            xo[:, 8 * j:8 * (j + 1), :],
                            xTown[8 * j * 128:8 * (j + 1) * 128, :].rearrange(
                                "(a p) u -> p a u", p=128)
                        ).then_inc(SS[0].h, 16)
                for o in range(DT):
                    if o == 0:
                        g.wait_ge(s_t2.h, T2_0 + DT)
                    if o >= 4:
                        g.wait_ge(s_t1.h, TQ_0 + o - 3)
                    g.dma_start(
                        wqs[o % 4][:],
                        wqT[:, o * 128:(o + 1) * 128].rearrange(
                            "(a p) o -> p a o", p=128)
                    ).then_inc(WQS[o % 4].h, 16)

            @blk.tensor
            def _(t):
                for i in range(NKC):
                    if i == 0:
                        t.wait_ge(SMg.h, M0 + 16)
                    t.wait_ge(SG[i % 2].h, GB[i % 2] + 16 * (i // 2 + 1))
                    if i >= 1:
                        t.wait_ge(s_a1.h, A1_0 + 2 * i)
                    for a in range(DT):
                        t.matmul(kp[:, :], wkv[:, a, 0:128], xtmp[i % 2][:, a, :],
                                 start=(a == 0), stop=(a == DT - 1))
                    for a in range(DT):
                        mm = t.matmul(vp[:, :], wkv[:, a, 128:256],
                                      xtmp[i % 2][:, a, :],
                                      start=(a == 0), stop=(a == DT - 1))
                    mm.then_inc(s_t1.h, 1)
                for j in range(DT):
                    t.wait_ge(s_a1.h, A1_0 + 2 * ((j * 128) // KCH) + 2)
                    if j >= 2:
                        t.wait_ge(s_v1.h, V1_0 + j - 1)
                    t.transpose(tp[j % 2][:], VTfm[:, j * 128:(j + 1) * 128],
                                ident[:]).then_inc(s_t2.h, 1)
                for o in range(DT):
                    if o == 0:
                        t.wait_ge(SS[0].h, XO0 + 64)
                        t.wait_ge(s_a1.h, A1_0 + 2 * NKC)
                        t.wait_ge(s_v1.h, V1_0 + DT)
                    t.wait_ge(WQS[o % 4].h, WB[o % 4] + 16 * (o // 4 + 1))
                    if o >= 2:
                        t.wait_ge(s_a1.h, AQ_0 + o - 1)
                    for a in range(DT):
                        for nb in range(4):
                            mm = t.matmul(qp[o % 2][:, nb * 512:(nb + 1) * 512],
                                          wqs[o % 4][:, a, :],
                                          xo[:, a, nb * 512:(nb + 1) * 512],
                                          start=(a == 0), stop=(a == DT - 1))
                    mm.then_inc(s_t1.h, 1)

            @blk.scalar
            def _(sc):
                for i in range(NKC):
                    sc.wait_ge(s_t1.h, T1_0 + i + 1)
                    sc.activation(KT[:, i * KCH:(i + 1) * KCH], kp[:, :],
                                  AF.Identity, bias=bkv[:, 0:1]
                                  ).then_inc(s_a1.h, 1)
                    sc.activation(VTfm[:, i * KCH:(i + 1) * KCH], vp[:, :],
                                  AF.Identity, bias=bkv[:, 1:2]
                                  ).then_inc(s_a1.h, 1)
                for o in range(DT):
                    sc.wait_ge(s_t1.h, TQ_0 + o + 1)
                    if o >= 2:
                        sc.wait_ge(SO[o % 2].h, OB[o % 2] + 64 * (o // 2))
                    sc.activation(qst[o % 2][:], qp[o % 2][:], AF.Identity,
                                  bias=bq[:, o:o + 1]).then_inc(s_a1.h, 1)

            @blk.vector
            def _(v):
                for j in range(DT):
                    v.wait_ge(s_t2.h, T2_0 + j + 1)
                    v.tensor_copy(Vt[:, j, :], tp[j % 2][:]).then_inc(s_v1.h, 1)

            @blk.sync
            def _(sy):
                for o in range(DT):
                    sy.wait_ge(s_a1.h, AQ_0 + o + 1)
                    for c in range(NCH):
                        sy.dma_start(
                            qspc[c:c + 1, o:o + 1, :, :].rearrange(
                                "c o p q -> p (c o) q"),
                            qst[o % 2][:, c * CH:(c + 1) * CH]
                        ).then_inc(SO[o % 2].h, 16)
                sy.wait_ge(SO[0].h, OB[0] + 1024)
                sy.wait_ge(SO[1].h, OB[1] + 1024)

        SMg.n += 16
        SS[0].n += 64
        s_t1.n += NKC + DT
        s_t2.n += DT
        s_a1.n += 2 * NKC + DT
        s_v1.n += DT
        SG[0].n += 16 * (NKC // 2)
        SG[1].n += 16 * (NKC // 2)
        for s in range(4):
            WQS[s].n += 16 * len(range(s, DT, 4))
        SO[0].n += 1024
        SO[1].n += 1024
        es_xo.close()

        es_aor = ExitStack()
        aor = es_aor.enter_context(nc.sbuf_tensor("aor", [128, DT, TOWN], BF16))
        with ExitStack() as ph:
            QR = [ph.enter_context(nc.sbuf_tensor(f"qr{i}", [128, 8, CH], BF16))
                  for i in range(3)]
            mk = [ph.enter_context(nc.sbuf_tensor(f"mk{i}", [128, MW * CH], BF16))
                  for i in range(2)]
            pb = [ph.enter_context(nc.sbuf_tensor(f"pb{i}", [128, 1024], BF16))
                  for i in range(4)]
            rec = [ph.enter_context(nc.sbuf_tensor(f"rec{i}", [128, CH], F32))
                   for i in range(2)]
            sp = [ph.enter_context(nc.psum_tensor(f"sp{i}", [128, 1024], F32))
                  for i in range(2)]
            op = [ph.enter_context(nc.psum_tensor(f"op{i}", [128, CH], F32))
                  for i in range(2)]
            db = [ph.enter_context(nc.psum_tensor(f"db{i}", [128, CH], F32))
                  for i in range(2)]

            QRB = [SQR[0].n, SQR[1].n, SQR[2].n]
            MB = SMg.n
            T1_0, T5_0 = s_t1.n, s_t5.n
            A1_0 = s_a1.n
            G1_0 = s_g1.n
            V1_0, V2_0 = s_v1.n, s_v2.n

            n_groups = sum(e // 2 for e in EXTS[0]) * H
            assert n_groups == sum(e // 2 for e in EXTS[1]) * H
            NHEAD = NCH * H

            def par_tabs(par):
                GRP = [e // 2 for e in EXTS[par]]
                SCUM = [0]
                for cc in range(NCH):
                    SCUM.append(SCUM[-1] + H * GRP[cc])
                GCUM = []
                tot = 0
                for cc in range(NCH):
                    for h in range(H):
                        tot += GRP[cc]
                        GCUM.append(tot)
                return GRP, SCUM, GCUM

            with nc.Block("b2") as blk:
                @blk.sync
                def _(sy):
                    idx = sy.partition_id() % 2
                    for par in _par_iter(sy, idx):
                        GRP, SCUM, GCUM = par_tabs(par)
                        for l in range(4 * NCH):
                            c, hg = l // 4, l % 4
                            if l >= 3:
                                lc, lhg = (l - 3) // 4, (l - 3) % 4
                                sy.wait_ge(s_t1.h,
                                           T1_0 + SCUM[lc] + (lhg + 1) * 8 * GRP[lc])
                            sy.dma_start(
                                QR[l % 3][:],
                                qspc[c:c + 1, hg * 8:(hg + 1) * 8, :, :].rearrange(
                                    "c h p q -> p (c h) q")
                            ).then_inc(SQR[l % 3].h, 16)

                @blk.gpsimd
                def _(g):
                    idx = g.partition_id() % 2
                    g10 = s_g1.n
                    for par in _par_iter(g, idx):
                        s_g1.n = g10
                        GRP, SCUM, GCUM = par_tabs(par)
                        for c in range(2):
                            g.dma_start(
                                mk[c % 2][:],
                                masksd[c:c + 1, :, :].rearrange("c p w -> p (c w)")
                            ).then_inc(SMg.h, 16)
                        gg = 0
                        hd = 0
                        for c in range(NCH):
                            Gc = GRP[c]
                            for h in range(H):
                                if h == 0:
                                    g.wait_ge(SMg.h, MB + 16 * (c + 1))
                                for gx in (Gc - 2, Gc - 1):
                                    pj = gg + gx
                                    w0 = 2 * (gx - (Gc - 2))
                                    g.wait_ge(s_a1.h, A1_0 + pj + 1)
                                    g.tensor_mul(
                                        pb[pj % 4][:, 0:1024], pb[pj % 4][:, 0:1024],
                                        mk[c % 2][:, w0 * 512:(w0 + 2) * 512]
                                    ).then_inc(s_g1.h, 1)
                                    s_g1.n += 1
                                gg += Gc
                                hd += 1
                            if c + 2 < NCH:
                                g.wait_ge(s_g1.h, s_g1.n)
                                g.dma_start(
                                    mk[c % 2][:],
                                    masksd[c + 2:c + 3, :, :].rearrange(
                                        "c p w -> p (c w)")
                                ).then_inc(SMg.h, 16)

                @blk.scalar
                def _(sc):
                    idx = sc.partition_id() % 2
                    for par in _par_iter(sc, idx):
                        GRP, SCUM, GCUM = par_tabs(par)
                        gg = 0
                        for c in range(NCH):
                            Gc = GRP[c]
                            for h in range(H):
                                for g in range(Gc):
                                    sc.wait_ge(s_t1.h, T1_0 + gg + 1)
                                    if gg >= 4:
                                        sc.wait_ge(s_t5.h, T5_0 + gg - 3)
                                    sc.activation(pb[gg % 4][:], sp[gg % 2][:], AF.Exp,
                                                  bias=eshift[:, 0:1], scale=SCALE
                                                  ).then_inc(s_a1.h, 1)
                                    gg += 1

                @blk.tensor
                def _(t):
                    idx = t.partition_id() % 2
                    for par in _par_iter(t, idx):
                        GRP, SCUM, GCUM = par_tabs(par)
                        gg = 0
                        hd = 0
                        for c in range(NCH):
                            Gc = GRP[c]
                            for h in range(H):
                                l = 4 * c + h // 8
                                if h % 8 == 0:
                                    t.wait_ge(SQR[l % 3].h,
                                              QRB[l % 3] + 16 * (l // 3 + 1))
                                qa_ap = QR[l % 3][:, h % 8, :]
                                first_pgg = gg

                                def pv_group(j, base_gg, Gc=None, hd=None):
                                    pj = base_gg + j
                                    masked = (j >= Gc - 2)
                                    if masked:
                                        t.wait_ge(s_g1.h,
                                                  G1_0 + 2 * hd + (j - (Gc - 2)) + 1)
                                    else:
                                        t.wait_ge(s_a1.h, A1_0 + pj + 1)
                                    if j == 0 and hd >= 2:
                                        t.wait_ge(s_v1.h, V1_0 + hd - 1)
                                    last = (j == Gc - 1)
                                    t.matmul(op[hd % 2][:, :], Vt[:, 2 * j, :],
                                             pb[pj % 4][:, 0:512],
                                             start=(j == 0), stop=False)
                                    t.matmul(op[hd % 2][:, :], Vt[:, 2 * j + 1, :],
                                             pb[pj % 4][:, 512:1024],
                                             start=False, stop=last)
                                    t.matmul(db[hd % 2][:, :], ones_mat[:],
                                             pb[pj % 4][:, 0:512],
                                             start=(j == 0), stop=False)
                                    t.matmul(db[hd % 2][:, :], ones_mat[:],
                                             pb[pj % 4][:, 512:1024],
                                             start=False, stop=last
                                             ).then_inc(s_t5.h, 1)

                                for g in range(Gc):
                                    if gg >= 2:
                                        t.wait_ge(s_a1.h, A1_0 + gg - 1)
                                    t.matmul(sp[gg % 2][:, 0:512],
                                             KT[:, (2 * g) * 128:(2 * g + 1) * 128],
                                             qa_ap, start=True, stop=True)
                                    t.matmul(sp[gg % 2][:, 512:1024],
                                             KT[:, (2 * g + 1) * 128:(2 * g + 2) * 128],
                                             qa_ap, start=True, stop=True
                                             ).then_inc(s_t1.h, 1)
                                    gg += 1
                                    if g >= 1:
                                        pv_group(g - 1, first_pgg, Gc=Gc, hd=hd)
                                pv_group(Gc - 1, first_pgg, Gc=Gc, hd=hd)
                                hd += 1

                @blk.vector
                def _(v):
                    idx = v.partition_id() % 2
                    for par in _par_iter(v, idx):
                        GRP, SCUM, GCUM = par_tabs(par)
                        hd = 0
                        for c in range(NCH):
                            for h in range(H):
                                v.wait_ge(s_t5.h, T5_0 + GCUM[hd])
                                v.reciprocal(rec[hd % 2][:], db[hd % 2][:]
                                             ).then_inc(s_v2.h, 1)
                                v.wait_ge(s_v2.h, V2_0 + hd + 1)
                                v.tensor_mul(aor[:, h, c * CH:(c + 1) * CH],
                                             op[hd % 2][:], rec[hd % 2][:]
                                             ).then_inc(s_v1.h, 1)
                                hd += 1

            SMg.n += 4 * 16
            SQR[0].n += 16 * len(range(0, 4 * NCH, 3))
            SQR[1].n += 16 * len(range(1, 4 * NCH, 3))
            SQR[2].n += 16 * len(range(2, 4 * NCH, 3))
            s_t1.n += n_groups
            s_t5.n += n_groups
            s_a1.n += n_groups
            s_g1.n = G1_0 + 2 * NHEAD
            s_v1.n += NHEAD
            s_v2.n += NHEAD

        ph3 = ExitStack()
        wos = [ph3.enter_context(nc.sbuf_tensor(f"wos{i}", [128, DT, 128], BF16))
               for i in range(2)]
        xres = ph3.enter_context(nc.sbuf_tensor("xres", [128, TOWN], F32))
        A1b = ph3.enter_context(nc.sbuf_tensor("accA", [128, TOWN], F32))
        SQb = ph3.enter_context(nc.sbuf_tensor("accSQ", [128, TOWN], F32))
        sqs = ph3.enter_context(nc.sbuf_tensor("sqs", [128, TOWN // 2], F32))
        pp3 = ExitStack()
        wp = [pp3.enter_context(nc.psum_tensor(f"wp{i}", [128, TOWN], F32))
              for i in range(2)]
        ph3.close()
        pp3.close()
        es_aor.close()
        es_kv.close()

        es_y1 = ExitStack()
        y1 = es_y1.enter_context(nc.sbuf_tensor("y1", [128, DT, TOWN], BF16))
        es_w1 = ExitStack()
        w1s = [es_w1.enter_context(nc.sbuf_tensor(f"w1s{i}", [128, DT, 128], BF16))
               for i in range(4)]
        ph_ap = ExitStack()
        pr32 = [ph_ap.enter_context(nc.sbuf_tensor(f"pr32a{i}", [128, TOWN], F32))
                for i in range(2)]
        t_v = ph_ap.enter_context(nc.sbuf_tensor("t_v", [128, TOWN // 2], F32))
        t_g = ph_ap.enter_context(nc.sbuf_tensor("t_g", [128, TOWN // 2], F32))
        ph_ap.close()

        ph4 = ExitStack()
        sl = ph4.enter_context(nc.sbuf_tensor("sl", [128, 1024], F32))
        h2s = ph4.enter_context(nc.sbuf_tensor("h2s", [128, 1024], F32))
        x4st = [ph4.enter_context(nc.sbuf_tensor(f"x4st{i}", [128, TOWN], BF16))
                for i in range(2)]
        pp4 = ExitStack()
        hp = [pp4.enter_context(nc.psum_tensor(f"hp{i}", [128, 1024], F32))
              for i in range(4)]
        ph4.close()
        pp4.close()
        es_w1.close()
        es_y1.close()

        es_p2 = ExitStack()
        P2 = [es_p2.enter_context(nc.sbuf_tensor(f"p2st{i}", [128, TOWN], F32))
              for i in range(2)]
        ph5 = ExitStack()
        x4r = ph5.enter_context(nc.sbuf_tensor("x4r", [128, DT, TOWN], BF16))
        w3s = [ph5.enter_context(nc.sbuf_tensor(f"w3s{i}", [128, DT, 128], BF16))
               for i in range(2)]
        pr32b = ph5.enter_context(nc.sbuf_tensor("pr32b", [128, TOWN], F32))
        sq2s = ph5.enter_context(nc.sbuf_tensor("sq2s", [128, TOWN // 2], F32))
        A2b = ph5.enter_context(nc.sbuf_tensor("accA2", [128, TOWN], F32))
        SQ2b = ph5.enter_context(nc.sbuf_tensor("accSQ2", [128, TOWN], F32))
        pp5 = ExitStack()
        wp5 = [pp5.enter_context(nc.psum_tensor(f"wp5{i}", [128, TOWN], F32))
               for i in range(2)]
        ph5.close()
        pp5.close()

        ph_l2 = ExitStack()
        pr32c = [ph_l2.enter_context(nc.sbuf_tensor(f"pr32c{i}", [128, TOWN], F32))
                 for i in range(3)]
        outst_v = [ph_l2.enter_context(nc.sbuf_tensor(f"ov{i}", [128, TOWN], F32))
                   for i in range(2)]
        outst_g = [ph_l2.enter_context(nc.sbuf_tensor(f"og{i}", [128, TOWN], F32))
                   for i in range(2)]
        t_v2 = ph_l2.enter_context(nc.sbuf_tensor("t_v2", [128, TOWN], F32))
        t_g2 = ph_l2.enter_context(nc.sbuf_tensor("t_g2", [128, TOWN], F32))
        ph_l2.close()
        es_p2.close()
        es_p1.close()

        RS1, MRS1 = P1[0], P1[1]
        RS2, MRS2 = P2[0], P2[1]
        HT = TOWN // 2

        GB3 = [SG[0].n, SG[1].n]
        SB3 = SS[0].n
        OB3 = [SO[0].n, SO[1].n]
        QR3 = [SQR[0].n, SQR[1].n, SQR[2].n]
        SM5 = SMs.n
        SMG5 = SMg.n
        VA = s_v1.n
        T1_3, A1_3, V1_3, V3_3, G1_3 = s_t1.n, s_a1.n, s_v1.n, s_v3.n, s_g1.n

        T1_S = T1_3 + DT
        V3_F0 = V3_3 + 5 * DT
        V3_FIN = V3_F0 + 8
        A1_SQT = A1_3 + DT + 1
        G1_A = G1_3 + DT
        NEV = len(range(0, DT, 2))
        NOD = len(range(1, DT, 2))
        V1_AP = V1_3
        G1_AP = G1_A

        T1_4 = T1_S + 1
        A1_4 = A1_SQT
        V1_4 = V1_AP + NEV
        G1_4 = G1_AP + 4 * NOD
        OB4 = [OB3[0] + 256, OB3[1] + 256]

        T1_5 = T1_4 + 2 * DT
        A1_5 = A1_4 + 4 * DT
        V1_L2 = V1_4 + 2 * DT
        G1_5 = G1_4
        G2_A = G1_5 + DT
        V3_AP = V3_FIN + 3 * NEV
        V3_5 = V3_AP
        V3_F2 = V3_5 + 10 * DT
        V3_FIN2 = V3_F2 + 8
        A2_SQT = A1_5 + DT + 1
        OB5 = [OB4[0] + 256, OB4[1] + 256]
        SB5 = SB3 + 512
        QR3b = [QR3[0] + 256, QR3[1] + 256, QR3[2]]
        G1_L2 = G2_A
        OB_F = [OB5[0] + 256, OB5[1] + 256]

        def apply_done_wait(eng, a):
            if a % 2 == 0:
                eng.wait_ge(s_v1.h, V1_AP + a // 2 + 1)
            else:
                eng.wait_ge(s_g1.h, G1_AP + 4 * (a // 2 + 1))

        with nc.Block("b3") as blk:
            @blk.tensor
            def _(t):
                for o in range(DT):
                    if o == 0:
                        t.wait_ge(s_v1.h, VA)
                    t.wait_ge(SG[o % 2].h, GB3[o % 2] + 16 * (o // 2 + 1))
                    if o >= 2:
                        t.wait_ge(s_a1.h, A1_3 + o - 1)
                    for a in range(DT):
                        for nb in range(4):
                            mm = t.matmul(wp[o % 2][:, nb * 512:(nb + 1) * 512],
                                          wos[o % 2][:, a, :],
                                          aor[:, a, nb * 512:(nb + 1) * 512],
                                          start=(a == 0), stop=(a == DT - 1))
                    mm.then_inc(s_t1.h, 1)
                t.wait_ge(s_a1.h, A1_3 + DT)
                t.wait_ge(s_g1.h, G1_A)
                t.wait_ge(s_v3.h, V3_F0)
                for j in range(4):
                    t.matmul(wp[0][:, j * 512:(j + 1) * 512], ones_f32[:],
                             A1b[:, j * 512:(j + 1) * 512], start=True, stop=True)
                for j in range(4):
                    mm = t.matmul(wp[1][:, j * 512:(j + 1) * 512], ones_f32[:],
                                  SQb[:, j * 512:(j + 1) * 512],
                                  start=True, stop=True)
                mm.then_inc(s_t1.h, 1)
                for i in range(DT):
                    t.wait_ge(SG[i % 2].h, GB3[i % 2] + 256 + 32 * (i // 2 + 1))
                    if i == 2:
                        t.wait_ge(s_v1.h, V1_AP + NEV)
                        t.wait_ge(s_g1.h, G1_AP + 4 * NOD)
                    for hf in range(2):
                        hi = 2 * i + hf
                        if hi >= 2:
                            t.wait_ge(s_a1.h, A1_4 + 2 * (hi - 1))
                        for a in range(DT):
                            if i <= 1 and hf == 0:
                                apply_done_wait(t, a)
                            for nb in range(2):
                                t.matmul(hp[hf][:, nb * 512:(nb + 1) * 512],
                                         w1s[2 * (i % 2)][:, a, :],
                                         y1[:, a, hf * 1024 + nb * 512:
                                            hf * 1024 + (nb + 1) * 512],
                                         start=(a == 0), stop=(a == DT - 1))
                        for a in range(DT):
                            for nb in range(2):
                                mm = t.matmul(hp[2 + hf][:, nb * 512:(nb + 1) * 512],
                                              w1s[2 * (i % 2) + 1][:, a, :],
                                              y1[:, a, hf * 1024 + nb * 512:
                                                 hf * 1024 + (nb + 1) * 512],
                                              start=(a == 0), stop=(a == DT - 1))
                        mm.then_inc(s_t1.h, 1)
                for o in range(DT):
                    if o == 0:
                        t.wait_ge(s_a1.h, A1_5)
                    t.wait_ge(SG[o % 2].h,
                              GB3[o % 2] + 256 + 512 + 16 * (o // 2 + 1))
                    if o >= 2:
                        t.wait_ge(s_a1.h, A1_5 + o - 1)
                    for a in range(DT):
                        if o == 0 and a % 4 == 0:
                            gq = a // 4
                            if gq % 2 == 0:
                                t.wait_ge(SMs.h, SM5 + 16 * (gq // 2 + 1))
                            else:
                                t.wait_ge(SMg.h, SMG5 + 16 * (gq // 2 + 1))
                        if o == 1 and a == 0:
                            t.wait_ge(SMs.h, SM5 + 64)
                            t.wait_ge(SMg.h, SMG5 + 64)
                        for nb in range(4):
                            mm = t.matmul(wp5[o % 2][:, nb * 512:(nb + 1) * 512],
                                          w3s[o % 2][:, a, :],
                                          x4r[:, a, nb * 512:(nb + 1) * 512],
                                          start=(a == 0), stop=(a == DT - 1))
                    mm.then_inc(s_t1.h, 1)
                t.wait_ge(s_a1.h, A1_5 + DT)
                t.wait_ge(s_g1.h, G2_A)
                t.wait_ge(s_v3.h, V3_F2)
                for j in range(4):
                    t.matmul(wp5[0][:, j * 512:(j + 1) * 512], ones_f32[:],
                             A2b[:, j * 512:(j + 1) * 512], start=True, stop=True)
                for j in range(4):
                    mm = t.matmul(wp5[1][:, j * 512:(j + 1) * 512], ones_f32[:],
                                  SQ2b[:, j * 512:(j + 1) * 512],
                                  start=True, stop=True)
                mm.then_inc(s_t1.h, 1)

            @blk.scalar
            def _(sc):
                sc.dma_start(xres[:], xTown[0:128, :]).then_inc(SS[0].h, 16)
                for o in range(DT):
                    sc.wait_ge(s_t1.h, T1_3 + o + 1)
                    if o >= 2:
                        sc.wait_ge(SO[o % 2].h, OB3[o % 2] + 16 * (o // 2))
                        sc.wait_ge(s_g1.h, G1_3 + o - 1)
                        sc.wait_ge(s_v3.h, V3_3 + 5 * (o - 2) + 4)
                    sc.activation(P1[o % 2][:], wp[o % 2][:], AF.Identity,
                                  bias=bo[:, o:o + 1]).then_inc(s_a1.h, 1)
                    if o + 1 < DT:
                        sc.wait_ge(s_v3.h, V3_3 + 5 * o + 1)
                        sc.dma_start(xres[:],
                                     xTown[(o + 1) * 128:(o + 2) * 128, :]
                                     ).then_inc(SS[0].h, 16)
                sc.wait_ge(s_v3.h, V3_F0 + 6)
                sc.activation(SQb[:], RS1[:], AF.Sqrt).then_inc(s_a1.h, 1)
                for i in range(DT):
                    for hf in range(2):
                        hi = 2 * i + hf
                        sc.wait_ge(s_t1.h, T1_4 + hi + 1)
                        if hi >= 1:
                            sc.wait_ge(s_v1.h, V1_4 + hi)
                        sc.activation(sl[:], hp[hf][:], AF.Silu,
                                      bias=b12[:, i:i + 1]).then_inc(s_a1.h, 1)
                        sc.activation(h2s[:], hp[2 + hf][:], AF.Identity,
                                      bias=b12[:, 32 + i:33 + i]).then_inc(s_a1.h, 1)
                sc.wait_ge(s_v1.h, V1_L2)
                sc.wait_ge(SO[0].h, OB4[0] + 256)
                sc.wait_ge(SO[1].h, OB4[1] + 256)
                for gq in range(1, 8, 2):
                    sc.dma_start(
                        x4r[:, 4 * gq:4 * (gq + 1), :],
                        x4sp[4 * gq * 128:4 * (gq + 1) * 128, :].rearrange(
                            "(a p) u -> p a u", p=128)
                    ).then_inc(SMg.h, 16)
                sc.dma_start(pr32b[:], p1sp[0:128, :]).then_inc(SS[0].h, 16)
                for o in range(DT):
                    sc.wait_ge(s_t1.h, T1_5 + o + 1)
                    if o >= 2:
                        sc.wait_ge(SO[o % 2].h, OB5[o % 2] + 16 * (o // 2))
                        sc.wait_ge(s_g1.h, G1_5 + o - 1)
                        sc.wait_ge(s_v3.h, V3_5 + 10 * (o - 2) + 9)
                    sc.activation(P2[o % 2][:], wp5[o % 2][:], AF.Identity,
                                  bias=b3[:, o:o + 1]).then_inc(s_a1.h, 1)
                    if o + 1 < DT:
                        sc.wait_ge(s_v3.h, V3_5 + 10 * o + 6)
                        sc.dma_start(pr32b[:],
                                     p1sp[(o + 1) * 128:(o + 2) * 128, :]
                                     ).then_inc(SS[0].h, 16)
                sc.wait_ge(s_v3.h, V3_F2 + 6)
                sc.activation(SQ2b[:], RS2[:], AF.Sqrt).then_inc(s_a1.h, 1)
                for a in range(1, DT, 2):
                    k = a // 2
                    sc.wait_ge(s_g1.h, G1_L2 + 2 * (k + 1))
                    sc.dma_start(outT[a * 128:(a + 1) * 128, :],
                                 outst_g[k % 2][:]).then_inc(SO[1].h, 16)
                sc.wait_ge(SO[1].h, OB_F[1] + 16 * NOD)

            @blk.gpsimd
            def _(g):
                for o in range(2):
                    g.dma_start(
                        wos[o][:],
                        woT[:, o * 128:(o + 1) * 128].rearrange(
                            "(a p) o -> p a o", p=128)
                    ).then_inc(SG[o].h, 16)
                for o in range(DT):
                    if o + 2 < DT:
                        g.wait_ge(s_t1.h, T1_3 + o + 1)
                        g.dma_start(
                            wos[o % 2][:],
                            woT[:, (o + 2) * 128:(o + 3) * 128].rearrange(
                                "(a p) o -> p a o", p=128)
                        ).then_inc(SG[o % 2].h, 16)
                    g.wait_ge(s_v3.h, V3_3 + 5 * o + 1)
                    if o == 0:
                        g.tensor_copy(A1b[:], P1[0][:]).then_inc(s_g1.h, 1)
                    else:
                        g.tensor_add(A1b[:], A1b[:], P1[o % 2][:]).then_inc(s_g1.h, 1)
                    g.wait_ge(s_g1.h, G1_3 + o + 1)
                g.wait_ge(s_t1.h, T1_S)
                for i in range(2):
                    for k, col in ((0, i), (1, 32 + i)):
                        g.dma_start(
                            w1s[2 * (i % 2) + k][:],
                            w12T[:, col * 128:(col + 1) * 128].rearrange(
                                "(a p) o -> p a o", p=128)
                        ).then_inc(SG[i % 2].h, 16)
                for a in range(1, DT, 2):
                    k = a // 2
                    g.wait_ge(SQR[a % 2].h, QR3[a % 2] + 16 * (k + 1))
                    if k == 0:
                        g.wait_ge(s_v3.h, V3_FIN)
                    else:
                        g.wait_ge(s_g1.h, G1_AP + 4 * k)
                    for h in range(2):
                        g.tensor_mul(t_g[:], pr32[a % 2][:, h * HT:(h + 1) * HT],
                                     RS1[:, h * HT:(h + 1) * HT]).then_inc(s_g1.h, 1)
                        g.wait_ge(s_g1.h, G1_AP + 4 * k + 2 * h + 1)
                        g.tensor_sub(y1[:, a, h * HT:(h + 1) * HT], t_g[:],
                                     MRS1[:, h * HT:(h + 1) * HT]
                                     ).then_inc(s_g1.h, 1)
                        g.wait_ge(s_g1.h, G1_AP + 4 * k + 2 * h + 2)
                for i in range(2, DT):
                    g.wait_ge(s_t1.h, T1_4 + 2 * (i - 1))
                    for k, col in ((0, i), (1, 32 + i)):
                        g.dma_start(
                            w1s[2 * (i % 2) + k][:],
                            w12T[:, col * 128:(col + 1) * 128].rearrange(
                                "(a p) o -> p a o", p=128)
                        ).then_inc(SG[i % 2].h, 16)
                g.wait_ge(s_t1.h, T1_4 + 2 * DT)
                for gq in range(0, 8, 2):
                    g.wait_ge(SO[0].h, OB4[0] + 16 * (2 * gq + 2))
                    g.wait_ge(SO[1].h, OB4[1] + 16 * (2 * gq + 2))
                    g.dma_start(
                        x4r[:, 4 * gq:4 * (gq + 1), :],
                        x4sp[4 * gq * 128:4 * (gq + 1) * 128, :].rearrange(
                            "(a p) u -> p a u", p=128)
                    ).then_inc(SMs.h, 16)
                for o in range(2):
                    g.dma_start(
                        w3s[o][:],
                        w3T[:, o * 128:(o + 1) * 128].rearrange(
                            "(a p) o -> p a o", p=128)
                    ).then_inc(SG[o].h, 16)
                for o in range(DT):
                    if o + 2 < DT:
                        g.wait_ge(s_t1.h, T1_5 + o + 1)
                        g.dma_start(
                            w3s[o % 2][:],
                            w3T[:, (o + 2) * 128:(o + 3) * 128].rearrange(
                                "(a p) o -> p a o", p=128)
                        ).then_inc(SG[o % 2].h, 16)
                    g.wait_ge(s_v3.h, V3_5 + 10 * o + 8)
                    if o == 0:
                        g.tensor_copy(A2b[:], P2[0][:]).then_inc(s_g1.h, 1)
                    else:
                        g.tensor_add(A2b[:], A2b[:], P2[o % 2][:]).then_inc(s_g1.h, 1)
                    g.wait_ge(s_g1.h, G1_5 + o + 1)
                for a in range(1, DT, 2):
                    k = a // 2
                    g.wait_ge(SQR[a % 3].h, QR3b[a % 3] + 16 * (a // 3 + 1))
                    if k == 0:
                        g.wait_ge(s_v3.h, V3_FIN2)
                    else:
                        g.wait_ge(s_g1.h, G1_L2 + 2 * k)
                    if a >= 5:
                        g.wait_ge(SO[1].h, OB_F[1] + 16 * (k - 1))
                    g.tensor_mul(t_g2[:], pr32c[a % 3][:], RS2[:]).then_inc(s_g1.h, 1)
                    g.wait_ge(s_g1.h, G1_L2 + 2 * k + 1)
                    g.tensor_sub(outst_g[k % 2][:], t_g2[:], MRS2[:]
                                 ).then_inc(s_g1.h, 1)

            @blk.vector
            def _(v):
                def fence(instr):
                    instr.then_inc(s_v3.h, 1)
                    s_v3.n += 1
                    v.wait_ge(s_v3.h, s_v3.n)

                for o in range(DT):
                    v.wait_ge(s_a1.h, A1_3 + o + 1)
                    v.wait_ge(SS[0].h, SB3 + 16 * (o + 1))
                    fence(v.tensor_add(P1[o % 2][:], P1[o % 2][:], xres[:]))
                    for h in range(2):
                        fence(v.tensor_mul(sqs[:], P1[o % 2][:, h * HT:(h + 1) * HT],
                                           P1[o % 2][:, h * HT:(h + 1) * HT]))
                        if o == 0:
                            fence(v.tensor_copy(SQb[:, h * HT:(h + 1) * HT], sqs[:]))
                        else:
                            fence(v.tensor_add(SQb[:, h * HT:(h + 1) * HT],
                                               SQb[:, h * HT:(h + 1) * HT], sqs[:]))
                assert s_v3.n == V3_F0
                v.wait_ge(s_t1.h, T1_S + 1)
                v.wait_ge(SO[0].h, OB3[0] + 256)
                v.wait_ge(SO[1].h, OB3[1] + 256)
                v.wait_ge(s_g1.h, G1_A)
                fence(v.tensor_scalar_mul(A1b[:], wp[0][:], 1.0 / D))
                fence(v.tensor_scalar_mul(SQb[:], wp[1][:], 1.0 / D))
                fence(v.tensor_mul(RS1[:], A1b[:], A1b[:]))
                fence(v.tensor_sub(SQb[:], SQb[:], RS1[:]))
                fence(v.tensor_scalar_add(SQb[:], SQb[:], EPS))
                fence(v.reciprocal(RS1[:], SQb[:]))
                v.wait_ge(s_a1.h, A1_SQT)
                fence(v.tensor_copy(RS1[:], SQb[:]))
                fence(v.tensor_mul(MRS1[:], A1b[:], RS1[:]))
                assert s_v3.n == V3_FIN
                for a in range(0, DT, 2):
                    k = a // 2
                    v.wait_ge(SQR[a % 2].h, QR3[a % 2] + 16 * (k + 1))
                    if k >= 1:
                        v.wait_ge(s_v1.h, V1_AP + k)
                    for h in range(2):
                        fence(v.tensor_mul(t_v[:], pr32[a % 2][:, h * HT:(h + 1) * HT],
                                           RS1[:, h * HT:(h + 1) * HT]))
                        if h == 0:
                            fence(v.tensor_sub(y1[:, a, 0:HT], t_v[:],
                                               MRS1[:, 0:HT]))
                        else:
                            v.tensor_sub(y1[:, a, HT:TOWN], t_v[:],
                                         MRS1[:, HT:TOWN]).then_inc(s_v1.h, 1)
                assert s_v3.n == V3_AP
                for i in range(DT):
                    for hf in range(2):
                        hi = 2 * i + hf
                        v.wait_ge(s_a1.h, A1_4 + 2 * (hi + 1))
                        if hf == 0 and i >= 2:
                            v.wait_ge(SO[i % 2].h, OB4[i % 2] + 16 * (i // 2))
                        v.tensor_mul(x4st[i % 2][:, hf * 1024:(hf + 1) * 1024],
                                     sl[:], h2s[:]).then_inc(s_v1.h, 1)
                for o in range(DT):
                    v.wait_ge(SS[0].h, SB5 + 16 * (o + 1))
                    v.wait_ge(s_a1.h, A1_5 + o + 1)
                    for h in range(2):
                        hs = slice(h * HT, (h + 1) * HT)
                        fence(v.tensor_mul(sq2s[:], pr32b[:, hs], RS1[:, hs]))
                        fence(v.tensor_sub(sq2s[:], sq2s[:], MRS1[:, hs]))
                        fence(v.tensor_add(P2[o % 2][:, hs], P2[o % 2][:, hs],
                                           sq2s[:]))
                        fence(v.tensor_mul(sq2s[:], P2[o % 2][:, hs],
                                           P2[o % 2][:, hs]))
                        if o == 0:
                            fence(v.tensor_copy(SQ2b[:, hs], sq2s[:]))
                        else:
                            fence(v.tensor_add(SQ2b[:, hs], SQ2b[:, hs], sq2s[:]))
                assert s_v3.n == V3_F2
                v.wait_ge(s_t1.h, T1_5 + DT + 1)
                v.wait_ge(SO[0].h, OB5[0] + 256)
                v.wait_ge(SO[1].h, OB5[1] + 256)
                v.wait_ge(s_g1.h, G2_A)
                fence(v.tensor_scalar_mul(A2b[:], wp5[0][:], 1.0 / D))
                fence(v.tensor_scalar_mul(SQ2b[:], wp5[1][:], 1.0 / D))
                fence(v.tensor_mul(RS2[:], A2b[:], A2b[:]))
                fence(v.tensor_sub(SQ2b[:], SQ2b[:], RS2[:]))
                fence(v.tensor_scalar_add(SQ2b[:], SQ2b[:], EPS))
                fence(v.reciprocal(RS2[:], SQ2b[:]))
                v.wait_ge(s_a1.h, A2_SQT)
                fence(v.tensor_copy(RS2[:], SQ2b[:]))
                fence(v.tensor_mul(MRS2[:], A2b[:], RS2[:]))
                assert s_v3.n == V3_FIN2
                for a in range(0, DT, 2):
                    k = a // 2
                    v.wait_ge(SQR[a % 3].h, QR3b[a % 3] + 16 * (a // 3 + 1))
                    if k >= 1:
                        v.wait_ge(s_v1.h, V1_L2 + k)
                    if k >= 2:
                        v.wait_ge(SO[0].h, OB_F[0] + 16 * (k - 1))
                    fence(v.tensor_mul(t_v2[:], pr32c[a % 3][:], RS2[:]))
                    v.tensor_sub(outst_v[k % 2][:], t_v2[:], MRS2[:]
                                 ).then_inc(s_v1.h, 1)

            @blk.sync
            def _(sy):
                for o in range(DT):
                    sy.wait_ge(s_v3.h, V3_3 + 5 * o + 1)
                    sy.dma_start(p1sp[o * 128:(o + 1) * 128, :], P1[o % 2][:]
                                 ).then_inc(SO[o % 2].h, 16)
                for a in range(DT):
                    if a < 2:
                        sy.wait_ge(s_v3.h, V3_FIN)
                    else:
                        apply_done_wait(sy, a - 2)
                    sy.dma_start(pr32[a % 2][:], p1sp[a * 128:(a + 1) * 128, :]
                                 ).then_inc(SQR[a % 2].h, 16)
                for i in range(DT):
                    sy.wait_ge(s_v1.h, V1_4 + 2 * (i + 1))
                    sy.dma_start(x4sp[i * 128:(i + 1) * 128, :], x4st[i % 2][:]
                                 ).then_inc(SO[i % 2].h, 16)
                for o in range(DT):
                    sy.wait_ge(s_v3.h, V3_5 + 10 * o + 8)
                    sy.dma_start(p2sp[o * 128:(o + 1) * 128, :], P2[o % 2][:]
                                 ).then_inc(SO[o % 2].h, 16)
                for a in range(DT):
                    if a < 3:
                        sy.wait_ge(s_v3.h, V3_FIN2)
                    else:
                        am = a - 3
                        if am % 2 == 0:
                            sy.wait_ge(s_v1.h, V1_L2 + am // 2 + 1)
                        else:
                            sy.wait_ge(s_g1.h, G1_L2 + 2 * (am // 2 + 1))
                    sy.dma_start(pr32c[a % 3][:], p2sp[a * 128:(a + 1) * 128, :]
                                 ).then_inc(SQR[a % 3].h, 16)
                    if a >= 3 and (a - 3) % 2 == 0:
                        k = (a - 3) // 2
                        sy.dma_start(outT[(a - 3) * 128:(a - 2) * 128, :],
                                     outst_v[k % 2][:]).then_inc(SO[0].h, 16)
                sy.wait_ge(s_v1.h, V1_L2 + NEV)
                sy.dma_start(outT[30 * 128:31 * 128, :],
                             outst_v[15 % 2][:]).then_inc(SO[0].h, 16)
                sy.wait_ge(SO[0].h, OB_F[0] + 16 * NEV)

        SG[0].n = GB3[0] + 256 + 512 + 256
        SG[1].n = GB3[1] + 256 + 512 + 256
        SS[0].n = SB3 + 512 + 512
        SO[0].n = OB_F[0] + 16 * NEV
        SO[1].n = OB_F[1] + 16 * NOD
        SMs.n = SM5 + 64
        SMg.n = SMG5 + 64
        SQR[0].n = QR3[0] + 256 + 16 * 11
        SQR[1].n = QR3[1] + 256 + 16 * 11
        SQR[2].n = QR3[2] + 16 * 10
        s_t1.n = T1_5 + DT + 1
        s_a1.n = A2_SQT
        s_g1.n = G1_L2 + 2 * NOD
        s_v1.n = V1_L2 + NEV
        assert s_v3.n == V3_FIN2 + NEV

    return nc


def _make_masks(parity):
    masks = np.zeros((NCH, 128, MW * CH), dtype=np.float32)
    q = np.arange(CH)[None, :]
    k = np.arange(128)[:, None]
    for c in range(NCH):
        q0 = Q0S[parity][c]
        for w in range(MW):
            kt = EXTS[parity][c] - MW + w
            keep = (q0 + q) >= (kt * 128 + k)
            masks[c, :, w * CH:(w + 1) * CH] = np.where(keep, 1.0, 0.0)
    return masks


def kernel(L0_attn_x, L0_attn_wq_w, L0_attn_wq_b, L0_attn_wk_w, L0_attn_wk_b,
           L0_attn_wv_w, L0_attn_wv_b, L0_attn_wo_w, L0_attn_wo_b,
           L0_ffn_w12, L0_ffn_b12, L0_ffn_w3, L0_ffn_b3):
    x = np.asarray(L0_attn_x, dtype=np.float32)

    c = np.ascontiguousarray
    wqT = c(np.asarray(L0_attn_wq_w, np.float32).T)
    wkvT = c(np.concatenate([np.asarray(L0_attn_wk_w, np.float32).T,
                             np.asarray(L0_attn_wv_w, np.float32).T], axis=1))
    woT = c(np.asarray(L0_attn_wo_w, np.float32).T)
    w12T = c(np.asarray(L0_ffn_w12, np.float32).T)
    w3T = c(np.asarray(L0_ffn_w3, np.float32).T)
    bq = c(np.asarray(L0_attn_wq_b, np.float32).reshape(32, 128).T)
    bo = c(np.asarray(L0_attn_wo_b, np.float32).reshape(32, 128).T)
    b12 = c(np.asarray(L0_ffn_b12, np.float32).reshape(64, 128).T)
    b3 = c(np.asarray(L0_ffn_b3, np.float32).reshape(32, 128).T)
    bkv = c(np.stack([np.asarray(L0_attn_wk_b, np.float32),
                      np.asarray(L0_attn_wv_b, np.float32)], axis=1))
    ident = np.eye(128, dtype=np.float32)
    masks = [_make_masks(0), _make_masks(1)]

    nc = build_program()

    in_maps = []
    for core in range(8):
        b = core // 2
        parity = core % 2
        xTb = c(x[b].T)
        if parity == 0:
            xTown = c(np.concatenate([xTb[:, 0:1024], xTb[:, 3072:4096]], axis=1))
        else:
            xTown = c(xTb[:, 1024:3072])
        in_maps.append({
            "xT": xTb, "xTown": xTown, "wqT": wqT, "wkvT": wkvT, "woT": woT,
            "w12T": w12T, "w3T": w3T, "bqw": bq, "bow": bo, "b12w": b12, "b3w": b3,
            "bkvw": bkv, "masks": masks[parity], "identw": ident,
        })

    res = run_bass_kernel_spmd(nc, in_maps, core_ids=list(range(8)))
    kernel._last_results = res

    out = np.empty((BATCH, T, D), dtype=np.float32)
    for core in range(8):
        b = core // 2
        parity = core % 2
        oT = res.results[core]["outT"]
        if parity == 0:
            out[b, 0:1024, :] = oT[:, 0:1024].T
            out[b, 3072:4096, :] = oT[:, 1024:2048].T
        else:
            out[b, 1024:3072, :] = oT.T
    return out


# version 11
# speedup vs baseline: 1.2462x; 1.2462x over previous
from contextlib import ExitStack

import numpy as np
import concourse.bass as bass
import concourse.mybir as mybir
from concourse.bass_utils import run_bass_kernel_spmd

F32 = mybir.dt.float32
BF16 = mybir.dt.bfloat16
AF = mybir.ActivationFunctionType

D = 4096
T = 4096
BATCH = 4
H = 32
DT = D // 128
TOWN = 2048
CH = 512
NCH = TOWN // CH
M2 = 8192
EXTS = [[4, 8, 28, 32], [12, 16, 20, 24]]
MW = 4
Q0S = [[0, 512, 3072, 3584], [1024, 1536, 2048, 2560]]
SCALE = float(np.float32(1.0) / np.float32(np.sqrt(np.float32(128.0))))
ESHIFT = -12.0
EPS = 1e-5


PAR_OVERRIDE = None


def _par_iter(eng, idx):
    if PAR_OVERRIDE is not None:
        return [PAR_OVERRIDE]
    return eng.Switch(idx, 2)


class Sem:
    def __init__(self, nc, es, name):
        self.h = es.enter_context(nc.semaphore(name))
        self.n = 0


def build_program():
    nc = bass.Bass("TRN2", target_bir_lowering=False)

    xT = nc.dram_tensor("xT", [D, T], F32, kind="ExternalInput")
    xTown = nc.dram_tensor("xTown", [D, TOWN], F32, kind="ExternalInput")
    wqT = nc.dram_tensor("wqT", [D, D], F32, kind="ExternalInput")
    wkvT = nc.dram_tensor("wkvT", [D, 256], F32, kind="ExternalInput")
    woT = nc.dram_tensor("woT", [D, D], F32, kind="ExternalInput")
    w12T = nc.dram_tensor("w12T", [D, M2], F32, kind="ExternalInput")
    w3T = nc.dram_tensor("w3T", [D, D], F32, kind="ExternalInput")
    bqd = nc.dram_tensor("bqw", [128, 32], F32, kind="ExternalInput")
    bod = nc.dram_tensor("bow", [128, 32], F32, kind="ExternalInput")
    b12d = nc.dram_tensor("b12w", [128, 64], F32, kind="ExternalInput")
    b3d = nc.dram_tensor("b3w", [128, 32], F32, kind="ExternalInput")
    bkvd = nc.dram_tensor("bkvw", [128, 2], F32, kind="ExternalInput")
    masksd = nc.dram_tensor("masks", [NCH, 128, MW * CH], F32, kind="ExternalInput")
    identd = nc.dram_tensor("identw", [128, 128], F32, kind="ExternalInput")

    qspc = nc.dram_tensor("qspc", [NCH, H, 128, CH], BF16, kind="Internal")
    p1sp = nc.dram_tensor("p1sp", [D, TOWN], F32, kind="Internal")
    x4sp = nc.dram_tensor("x4sp", [D, TOWN], BF16, kind="Internal")
    p2sp = nc.dram_tensor("p2sp", [D, TOWN], F32, kind="Internal")
    outT = nc.dram_tensor("outT", [D, TOWN], F32, kind="ExternalOutput")

    with ExitStack() as es:
        ones_mat = es.enter_context(nc.sbuf_tensor("ones_mat", [128, 128], BF16))
        ones_f32 = es.enter_context(nc.sbuf_tensor("ones_f32", [128, 128], F32))
        eshift = es.enter_context(nc.sbuf_tensor("eshift", [128, 1], F32))
        ident = es.enter_context(nc.sbuf_tensor("ident", [128, 128], BF16))
        bq = es.enter_context(nc.sbuf_tensor("bq", [128, 32], F32))
        bo = es.enter_context(nc.sbuf_tensor("bo", [128, 32], F32))
        b12 = es.enter_context(nc.sbuf_tensor("b12", [128, 64], F32))
        b3 = es.enter_context(nc.sbuf_tensor("b3", [128, 32], F32))
        bkv = es.enter_context(nc.sbuf_tensor("bkv", [128, 2], F32))

        SG = [Sem(nc, es, "s_ga"), Sem(nc, es, "s_gb")]
        SS = [Sem(nc, es, "s_sa"), Sem(nc, es, "s_sb")]
        SO = [Sem(nc, es, "s_oa"), Sem(nc, es, "s_ob")]
        SQ = [Sem(nc, es, "s_qa"), Sem(nc, es, "s_qb")]
        SQR = [Sem(nc, es, "s_qr0"), Sem(nc, es, "s_qr1"), Sem(nc, es, "s_qr2")]
        SMg = Sem(nc, es, "s_mg")
        SMs = Sem(nc, es, "s_ms")
        s_t1 = Sem(nc, es, "s_t1")
        s_t2 = Sem(nc, es, "s_t2")
        s_t5 = Sem(nc, es, "s_t5")
        s_a1 = Sem(nc, es, "s_a1")
        s_g1 = Sem(nc, es, "s_g1")
        s_v1 = Sem(nc, es, "s_v1")
        s_v2 = Sem(nc, es, "s_v2")
        s_v3 = Sem(nc, es, "s_v3")

        Mg0, Ms0 = SMg.n, SMs.n
        with nc.Block("consts") as blk:
            @blk.gpsimd
            def _(g):
                g.dma_start(ident[:], identd[:, :]).then_inc(SMg.h, 16)
                g.memset(ones_mat[:], 1.0)
                g.memset(ones_f32[:], 1.0)
                g.memset(eshift[:], ESHIFT)

            @blk.scalar
            def _(sc):
                for dst, src in ((bq, bqd), (bo, bod), (b12, b12d), (b3, b3d),
                                 (bkv, bkvd)):
                    sc.dma_start(dst[:], src[:, :]).then_inc(SMs.h, 16)
                sc.wait_ge(SMs.h, Ms0 + 80)

            @blk.sync
            def _(sy):
                sy.wait_ge(SMg.h, Mg0 + 16)
        SMg.n += 16
        SMs.n += 80

        es_p1 = ExitStack()
        P1 = [es_p1.enter_context(nc.sbuf_tensor(f"p1st{i}", [128, TOWN], F32))
              for i in range(2)]
        es_kv = ExitStack()
        KT = es_kv.enter_context(nc.sbuf_tensor("KT", [128, T], BF16))
        Vt = es_kv.enter_context(nc.sbuf_tensor("Vt", [128, DT, 128], BF16))
        es_xo = ExitStack()
        xo = es_xo.enter_context(nc.sbuf_tensor("xo", [128, DT, TOWN], BF16))

        KCH = 128
        NKC = T // KCH
        ph0 = ExitStack()
        VTfm = ph0.enter_context(nc.sbuf_tensor("VTfm", [128, T], BF16))
        wkv = ph0.enter_context(nc.sbuf_tensor("wkv", [128, DT, 256], BF16))
        xtmp = [ph0.enter_context(nc.sbuf_tensor(f"xtmp{i}", [128, DT, KCH], BF16))
                for i in range(2)]
        pp0 = ExitStack()
        kp = pp0.enter_context(nc.psum_tensor("kp", [128, KCH], F32))
        vp = pp0.enter_context(nc.psum_tensor("vp", [128, KCH], F32))
        tp = [pp0.enter_context(nc.psum_tensor(f"tp{i}", [128, 128], BF16))
              for i in range(2)]
        ph0.close()
        pp0.close()
        ph1 = ExitStack()
        wqs = [ph1.enter_context(nc.sbuf_tensor(f"wqs{i}", [128, DT, 128], BF16))
               for i in range(4)]
        qst = [ph1.enter_context(nc.sbuf_tensor(f"qst{i}", [128, TOWN], BF16))
               for i in range(2)]
        pp1 = ExitStack()
        qp = [pp1.enter_context(nc.psum_tensor(f"qp{i}", [128, TOWN], F32))
              for i in range(2)]
        ph1.close()
        pp1.close()

        M0 = SMg.n
        GB = [SG[0].n, SG[1].n]
        XO0 = SS[0].n
        OB = [SO[0].n, SO[1].n]
        WQS = [SG[0], SG[1], SQ[0], SQ[1]]
        WB = [SG[0].n + 16 * (NKC // 2), SG[1].n + 16 * (NKC // 2),
              SQ[0].n, SQ[1].n]
        T1_0, T2_0, A1_0, V1_0 = s_t1.n, s_t2.n, s_a1.n, s_v1.n
        TQ_0 = T1_0 + NKC
        AQ_0 = A1_0 + 2 * NKC

        with nc.Block("b1") as blk:
            @blk.gpsimd
            def _(g):
                g.dma_start(wkv[:], wkvT[:, :].rearrange("(a p) o -> p a o", p=128)
                            ).then_inc(SMg.h, 16)
                for i in range(NKC):
                    if i >= 2:
                        g.wait_ge(s_t1.h, T1_0 + i - 1)
                    g.dma_start(
                        xtmp[i % 2][:],
                        xT[:, i * KCH:(i + 1) * KCH].rearrange(
                            "(a p) t -> p a t", p=128)
                    ).then_inc(SG[i % 2].h, 16)
                    if i % 8 == 7:
                        j = i // 8
                        g.dma_start(
                            xo[:, 8 * j:8 * (j + 1), :],
                            xTown[8 * j * 128:8 * (j + 1) * 128, :].rearrange(
                                "(a p) u -> p a u", p=128)
                        ).then_inc(SS[0].h, 16)
                for o in range(DT):
                    if o == 0:
                        g.wait_ge(s_t2.h, T2_0 + DT)
                    if o >= 4:
                        g.wait_ge(s_t1.h, TQ_0 + o - 3)
                    g.dma_start(
                        wqs[o % 4][:],
                        wqT[:, o * 128:(o + 1) * 128].rearrange(
                            "(a p) o -> p a o", p=128)
                    ).then_inc(WQS[o % 4].h, 16)

            @blk.tensor
            def _(t):
                for i in range(NKC):
                    if i == 0:
                        t.wait_ge(SMg.h, M0 + 16)
                    t.wait_ge(SG[i % 2].h, GB[i % 2] + 16 * (i // 2 + 1))
                    if i >= 1:
                        t.wait_ge(s_a1.h, A1_0 + 2 * i)
                    for a in range(DT):
                        t.matmul(kp[:, :], wkv[:, a, 0:128], xtmp[i % 2][:, a, :],
                                 start=(a == 0), stop=(a == DT - 1))
                    for a in range(DT):
                        mm = t.matmul(vp[:, :], wkv[:, a, 128:256],
                                      xtmp[i % 2][:, a, :],
                                      start=(a == 0), stop=(a == DT - 1))
                    mm.then_inc(s_t1.h, 1)
                for j in range(DT):
                    t.wait_ge(s_a1.h, A1_0 + 2 * ((j * 128) // KCH) + 2)
                    if j >= 2:
                        t.wait_ge(s_v1.h, V1_0 + j - 1)
                    t.transpose(tp[j % 2][:], VTfm[:, j * 128:(j + 1) * 128],
                                ident[:]).then_inc(s_t2.h, 1)
                for o in range(DT):
                    if o == 0:
                        t.wait_ge(SS[0].h, XO0 + 64)
                        t.wait_ge(s_a1.h, A1_0 + 2 * NKC)
                        t.wait_ge(s_v1.h, V1_0 + DT)
                    t.wait_ge(WQS[o % 4].h, WB[o % 4] + 16 * (o // 4 + 1))
                    if o >= 2:
                        t.wait_ge(s_a1.h, AQ_0 + o - 1)
                    for a in range(DT):
                        for nb in range(4):
                            mm = t.matmul(qp[o % 2][:, nb * 512:(nb + 1) * 512],
                                          wqs[o % 4][:, a, :],
                                          xo[:, a, nb * 512:(nb + 1) * 512],
                                          start=(a == 0), stop=(a == DT - 1))
                    mm.then_inc(s_t1.h, 1)

            @blk.scalar
            def _(sc):
                for i in range(NKC):
                    sc.wait_ge(s_t1.h, T1_0 + i + 1)
                    sc.activation(KT[:, i * KCH:(i + 1) * KCH], kp[:, :],
                                  AF.Identity, bias=bkv[:, 0:1]
                                  ).then_inc(s_a1.h, 1)
                    sc.activation(VTfm[:, i * KCH:(i + 1) * KCH], vp[:, :],
                                  AF.Identity, bias=bkv[:, 1:2]
                                  ).then_inc(s_a1.h, 1)
                for o in range(DT):
                    sc.wait_ge(s_t1.h, TQ_0 + o + 1)
                    if o >= 2:
                        sc.wait_ge(SO[o % 2].h, OB[o % 2] + 64 * (o // 2))
                    sc.activation(qst[o % 2][:], qp[o % 2][:], AF.Identity,
                                  bias=bq[:, o:o + 1]).then_inc(s_a1.h, 1)

            @blk.vector
            def _(v):
                for j in range(DT):
                    v.wait_ge(s_t2.h, T2_0 + j + 1)
                    v.tensor_copy(Vt[:, j, :], tp[j % 2][:]).then_inc(s_v1.h, 1)

            @blk.sync
            def _(sy):
                for o in range(DT):
                    sy.wait_ge(s_a1.h, AQ_0 + o + 1)
                    for c in range(NCH):
                        sy.dma_start(
                            qspc[c:c + 1, o:o + 1, :, :].rearrange(
                                "c o p q -> p (c o) q"),
                            qst[o % 2][:, c * CH:(c + 1) * CH]
                        ).then_inc(SO[o % 2].h, 16)
                sy.wait_ge(SO[0].h, OB[0] + 1024)
                sy.wait_ge(SO[1].h, OB[1] + 1024)

        SMg.n += 16
        SS[0].n += 64
        s_t1.n += NKC + DT
        s_t2.n += DT
        s_a1.n += 2 * NKC + DT
        s_v1.n += DT
        SG[0].n += 16 * (NKC // 2)
        SG[1].n += 16 * (NKC // 2)
        for s in range(4):
            WQS[s].n += 16 * len(range(s, DT, 4))
        SO[0].n += 1024
        SO[1].n += 1024
        es_xo.close()

        es_aor = ExitStack()
        aor = es_aor.enter_context(nc.sbuf_tensor("aor", [128, DT, TOWN], BF16))
        with ExitStack() as ph:
            QR = [ph.enter_context(nc.sbuf_tensor(f"qr{i}", [128, 8, CH], BF16))
                  for i in range(3)]
            mk = [ph.enter_context(nc.sbuf_tensor(f"mk{i}", [128, MW * CH], BF16))
                  for i in range(2)]
            pb = [ph.enter_context(nc.sbuf_tensor(f"pb{i}", [128, 1024], BF16))
                  for i in range(4)]
            rec = [ph.enter_context(nc.sbuf_tensor(f"rec{i}", [128, CH], F32))
                   for i in range(2)]
            sp = [ph.enter_context(nc.psum_tensor(f"sp{i}", [128, 1024], F32))
                  for i in range(2)]
            op = [ph.enter_context(nc.psum_tensor(f"op{i}", [128, CH], F32))
                  for i in range(2)]
            db = [ph.enter_context(nc.psum_tensor(f"db{i}", [128, CH], F32))
                  for i in range(2)]

            QRB = [SQR[0].n, SQR[1].n, SQR[2].n]
            MB = SMg.n
            T1_0, T5_0 = s_t1.n, s_t5.n
            A1_0 = s_a1.n
            G1_0 = s_g1.n
            V1_0, V2_0 = s_v1.n, s_v2.n

            n_groups = sum(e // 2 for e in EXTS[0]) * H
            assert n_groups == sum(e // 2 for e in EXTS[1]) * H
            NHEAD = NCH * H

            def par_tabs(par):
                GRP = [e // 2 for e in EXTS[par]]
                SCUM = [0]
                for cc in range(NCH):
                    SCUM.append(SCUM[-1] + H * GRP[cc])
                GCUM = []
                tot = 0
                for cc in range(NCH):
                    for h in range(H):
                        tot += GRP[cc]
                        GCUM.append(tot)
                return GRP, SCUM, GCUM

            with nc.Block("b2") as blk:
                @blk.sync
                def _(sy):
                    idx = sy.partition_id() % 2
                    for par in _par_iter(sy, idx):
                        GRP, SCUM, GCUM = par_tabs(par)
                        for l in range(4 * NCH):
                            c, hg = l // 4, l % 4
                            if l >= 3:
                                lc, lhg = (l - 3) // 4, (l - 3) % 4
                                sy.wait_ge(s_t1.h,
                                           T1_0 + SCUM[lc] + (lhg + 1) * 8 * GRP[lc])
                            sy.dma_start(
                                QR[l % 3][:],
                                qspc[c:c + 1, hg * 8:(hg + 1) * 8, :, :].rearrange(
                                    "c h p q -> p (c h) q")
                            ).then_inc(SQR[l % 3].h, 16)

                @blk.gpsimd
                def _(g):
                    idx = g.partition_id() % 2
                    for par in _par_iter(g, idx):
                        GRP, SCUM, GCUM = par_tabs(par)
                        for c in range(NCH):
                            if c >= 2:
                                g.wait_ge(s_t1.h, T1_0 + SCUM[c - 1])
                            g.dma_start(
                                mk[c % 2][:],
                                masksd[c:c + 1, :, :].rearrange("c p w -> p (c w)")
                            ).then_inc(SMg.h, 16)

                @blk.scalar
                def _(sc):
                    idx = sc.partition_id() % 2
                    for par in _par_iter(sc, idx):
                        GRP, SCUM, GCUM = par_tabs(par)
                        gg = 0
                        for c in range(NCH):
                            Gc = GRP[c]
                            for h in range(H):
                                for g in range(Gc):
                                    sc.wait_ge(s_t1.h, T1_0 + gg + 1)
                                    if gg >= 4:
                                        sc.wait_ge(s_t5.h, T5_0 + gg - 3)
                                    sc.activation(pb[gg % 4][:], sp[gg % 2][:], AF.Exp,
                                                  bias=eshift[:, 0:1], scale=SCALE
                                                  ).then_inc(s_a1.h, 1)
                                    gg += 1

                @blk.tensor
                def _(t):
                    idx = t.partition_id() % 2
                    for par in _par_iter(t, idx):
                        GRP, SCUM, GCUM = par_tabs(par)
                        gg = 0
                        hd = 0
                        for c in range(NCH):
                            Gc = GRP[c]
                            for h in range(H):
                                l = 4 * c + h // 8
                                if h % 8 == 0:
                                    t.wait_ge(SQR[l % 3].h,
                                              QRB[l % 3] + 16 * (l // 3 + 1))
                                qa_ap = QR[l % 3][:, h % 8, :]
                                first_pgg = gg

                                def pv_group(j, base_gg, Gc=None, hd=None):
                                    pj = base_gg + j
                                    t.wait_ge(s_a1.h, A1_0 + pj + 1)
                                    if j == 0 and hd >= 2:
                                        t.wait_ge(s_v1.h, V1_0 + hd - 1)
                                    last = (j == Gc - 1)
                                    t.matmul(op[hd % 2][:, :], Vt[:, 2 * j, :],
                                             pb[pj % 4][:, 0:512],
                                             start=(j == 0), stop=False)
                                    t.matmul(op[hd % 2][:, :], Vt[:, 2 * j + 1, :],
                                             pb[pj % 4][:, 512:1024],
                                             start=False, stop=last)
                                    t.matmul(db[hd % 2][:, :], ones_mat[:],
                                             pb[pj % 4][:, 0:512],
                                             start=(j == 0), stop=False)
                                    t.matmul(db[hd % 2][:, :], ones_mat[:],
                                             pb[pj % 4][:, 512:1024],
                                             start=False, stop=last
                                             ).then_inc(s_t5.h, 1)

                                for g in range(Gc):
                                    if gg >= 2:
                                        t.wait_ge(s_a1.h, A1_0 + gg - 1)
                                    masked = (g >= Gc - 2)
                                    if masked and g == Gc - 2 and h == 0:
                                        t.wait_ge(SMg.h, MB + 16 * (c + 1))
                                    if masked:
                                        w0 = 2 * (g - (Gc - 2))
                                        t.matmul(sp[gg % 2][:, 0:512],
                                                 KT[:, (2 * g) * 128:(2 * g + 1) * 128],
                                                 qa_ap, start=True, stop=False)
                                        t.matmul(sp[gg % 2][:, 0:512], ident[:],
                                                 mk[c % 2][:, w0 * 512:(w0 + 1) * 512],
                                                 start=False, stop=True)
                                        t.matmul(sp[gg % 2][:, 512:1024],
                                                 KT[:, (2 * g + 1) * 128:
                                                    (2 * g + 2) * 128],
                                                 qa_ap, start=True, stop=False)
                                        t.matmul(sp[gg % 2][:, 512:1024], ident[:],
                                                 mk[c % 2][:,
                                                    (w0 + 1) * 512:(w0 + 2) * 512],
                                                 start=False, stop=True
                                                 ).then_inc(s_t1.h, 1)
                                    else:
                                        t.matmul(sp[gg % 2][:, 0:512],
                                                 KT[:, (2 * g) * 128:(2 * g + 1) * 128],
                                                 qa_ap, start=True, stop=True)
                                        t.matmul(sp[gg % 2][:, 512:1024],
                                                 KT[:, (2 * g + 1) * 128:
                                                    (2 * g + 2) * 128],
                                                 qa_ap, start=True, stop=True
                                                 ).then_inc(s_t1.h, 1)
                                    gg += 1
                                    if g >= 1:
                                        pv_group(g - 1, first_pgg, Gc=Gc, hd=hd)
                                pv_group(Gc - 1, first_pgg, Gc=Gc, hd=hd)
                                hd += 1

                @blk.vector
                def _(v):
                    idx = v.partition_id() % 2
                    for par in _par_iter(v, idx):
                        GRP, SCUM, GCUM = par_tabs(par)
                        hd = 0
                        for c in range(NCH):
                            for h in range(H):
                                v.wait_ge(s_t5.h, T5_0 + GCUM[hd])
                                v.reciprocal(rec[hd % 2][:], db[hd % 2][:]
                                             ).then_inc(s_v2.h, 1)
                                v.wait_ge(s_v2.h, V2_0 + hd + 1)
                                v.tensor_mul(aor[:, h, c * CH:(c + 1) * CH],
                                             op[hd % 2][:], rec[hd % 2][:]
                                             ).then_inc(s_v1.h, 1)
                                hd += 1

            SMg.n += 4 * 16
            SQR[0].n += 16 * len(range(0, 4 * NCH, 3))
            SQR[1].n += 16 * len(range(1, 4 * NCH, 3))
            SQR[2].n += 16 * len(range(2, 4 * NCH, 3))
            s_t1.n += n_groups
            s_t5.n += n_groups
            s_a1.n += n_groups
            s_v1.n += NHEAD
            s_v2.n += NHEAD

        ph3 = ExitStack()
        wos = [ph3.enter_context(nc.sbuf_tensor(f"wos{i}", [128, DT, 128], BF16))
               for i in range(2)]
        xres = ph3.enter_context(nc.sbuf_tensor("xres", [128, TOWN], F32))
        A1b = ph3.enter_context(nc.sbuf_tensor("accA", [128, TOWN], F32))
        SQb = ph3.enter_context(nc.sbuf_tensor("accSQ", [128, TOWN], F32))
        sqs = ph3.enter_context(nc.sbuf_tensor("sqs", [128, TOWN // 2], F32))
        pp3 = ExitStack()
        wp = [pp3.enter_context(nc.psum_tensor(f"wp{i}", [128, TOWN], F32))
              for i in range(2)]
        ph3.close()
        pp3.close()
        es_aor.close()
        es_kv.close()

        es_y1 = ExitStack()
        y1 = es_y1.enter_context(nc.sbuf_tensor("y1", [128, DT, TOWN], BF16))
        es_w1 = ExitStack()
        w1s = [es_w1.enter_context(nc.sbuf_tensor(f"w1s{i}", [128, DT, 128], BF16))
               for i in range(4)]
        ph_ap = ExitStack()
        pr32 = [ph_ap.enter_context(nc.sbuf_tensor(f"pr32a{i}", [128, TOWN], F32))
                for i in range(2)]
        t_v = ph_ap.enter_context(nc.sbuf_tensor("t_v", [128, TOWN // 2], F32))
        t_g = ph_ap.enter_context(nc.sbuf_tensor("t_g", [128, TOWN // 2], F32))
        ph_ap.close()

        ph4 = ExitStack()
        sl = ph4.enter_context(nc.sbuf_tensor("sl", [128, 1024], F32))
        h2s = ph4.enter_context(nc.sbuf_tensor("h2s", [128, 1024], F32))
        x4st = [ph4.enter_context(nc.sbuf_tensor(f"x4st{i}", [128, TOWN], BF16))
                for i in range(2)]
        pp4 = ExitStack()
        hp = [pp4.enter_context(nc.psum_tensor(f"hp{i}", [128, 1024], F32))
              for i in range(4)]
        ph4.close()
        pp4.close()
        es_w1.close()
        es_y1.close()

        es_p2 = ExitStack()
        P2 = [es_p2.enter_context(nc.sbuf_tensor(f"p2st{i}", [128, TOWN], F32))
              for i in range(2)]
        ph5 = ExitStack()
        x4r = ph5.enter_context(nc.sbuf_tensor("x4r", [128, DT, TOWN], BF16))
        w3s = [ph5.enter_context(nc.sbuf_tensor(f"w3s{i}", [128, DT, 128], BF16))
               for i in range(2)]
        pr32b = ph5.enter_context(nc.sbuf_tensor("pr32b", [128, TOWN], F32))
        sq2s = ph5.enter_context(nc.sbuf_tensor("sq2s", [128, TOWN // 2], F32))
        A2b = ph5.enter_context(nc.sbuf_tensor("accA2", [128, TOWN], F32))
        SQ2b = ph5.enter_context(nc.sbuf_tensor("accSQ2", [128, TOWN], F32))
        pp5 = ExitStack()
        wp5 = [pp5.enter_context(nc.psum_tensor(f"wp5{i}", [128, TOWN], F32))
               for i in range(2)]
        ph5.close()
        pp5.close()

        ph_l2 = ExitStack()
        pr32c = [ph_l2.enter_context(nc.sbuf_tensor(f"pr32c{i}", [128, TOWN], F32))
                 for i in range(3)]
        outst_v = [ph_l2.enter_context(nc.sbuf_tensor(f"ov{i}", [128, TOWN], F32))
                   for i in range(2)]
        outst_g = [ph_l2.enter_context(nc.sbuf_tensor(f"og{i}", [128, TOWN], F32))
                   for i in range(2)]
        t_v2 = ph_l2.enter_context(nc.sbuf_tensor("t_v2", [128, TOWN], F32))
        t_g2 = ph_l2.enter_context(nc.sbuf_tensor("t_g2", [128, TOWN], F32))
        ph_l2.close()
        es_p2.close()
        es_p1.close()

        RS1, MRS1 = P1[0], P1[1]
        RS2, MRS2 = P2[0], P2[1]
        HT = TOWN // 2

        GB3 = [SG[0].n, SG[1].n]
        SB3 = SS[0].n
        OB3 = [SO[0].n, SO[1].n]
        QR3 = [SQR[0].n, SQR[1].n, SQR[2].n]
        SM5 = SMs.n
        SMG5 = SMg.n
        VA = s_v1.n
        T1_3, A1_3, V1_3, V3_3, G1_3 = s_t1.n, s_a1.n, s_v1.n, s_v3.n, s_g1.n

        T1_S = T1_3 + DT
        V3_F0 = V3_3 + 5 * DT
        V3_FIN = V3_F0 + 8
        A1_SQT = A1_3 + DT + 1
        G1_A = G1_3 + DT
        NEV = len(range(0, DT, 2))
        NOD = len(range(1, DT, 2))
        V1_AP = V1_3
        G1_AP = G1_A

        T1_4 = T1_S + 1
        A1_4 = A1_SQT
        V1_4 = V1_AP + NEV
        G1_4 = G1_AP + 4 * NOD
        OB4 = [OB3[0] + 256, OB3[1] + 256]

        T1_5 = T1_4 + 2 * DT
        A1_5 = A1_4 + 4 * DT
        V1_L2 = V1_4 + 2 * DT
        G1_5 = G1_4
        G2_A = G1_5 + DT
        V3_AP = V3_FIN + 3 * NEV
        V3_5 = V3_AP
        V3_F2 = V3_5 + 10 * DT
        V3_FIN2 = V3_F2 + 8
        A2_SQT = A1_5 + DT + 1
        OB5 = [OB4[0] + 256, OB4[1] + 256]
        SB5 = SB3 + 512
        QR3b = [QR3[0] + 256, QR3[1] + 256, QR3[2]]
        G1_L2 = G2_A
        OB_F = [OB5[0] + 256, OB5[1] + 256]

        def apply_done_wait(eng, a):
            if a % 2 == 0:
                eng.wait_ge(s_v1.h, V1_AP + a // 2 + 1)
            else:
                eng.wait_ge(s_g1.h, G1_AP + 4 * (a // 2 + 1))

        with nc.Block("b3") as blk:
            @blk.tensor
            def _(t):
                for o in range(DT):
                    if o == 0:
                        t.wait_ge(s_v1.h, VA)
                    t.wait_ge(SG[o % 2].h, GB3[o % 2] + 16 * (o // 2 + 1))
                    if o >= 2:
                        t.wait_ge(s_a1.h, A1_3 + o - 1)
                    for a in range(DT):
                        for nb in range(4):
                            mm = t.matmul(wp[o % 2][:, nb * 512:(nb + 1) * 512],
                                          wos[o % 2][:, a, :],
                                          aor[:, a, nb * 512:(nb + 1) * 512],
                                          start=(a == 0), stop=(a == DT - 1))
                    mm.then_inc(s_t1.h, 1)
                t.wait_ge(s_a1.h, A1_3 + DT)
                t.wait_ge(s_g1.h, G1_A)
                t.wait_ge(s_v3.h, V3_F0)
                for j in range(4):
                    t.matmul(wp[0][:, j * 512:(j + 1) * 512], ones_f32[:],
                             A1b[:, j * 512:(j + 1) * 512], start=True, stop=True)
                for j in range(4):
                    mm = t.matmul(wp[1][:, j * 512:(j + 1) * 512], ones_f32[:],
                                  SQb[:, j * 512:(j + 1) * 512],
                                  start=True, stop=True)
                mm.then_inc(s_t1.h, 1)
                for i in range(DT):
                    t.wait_ge(SG[i % 2].h, GB3[i % 2] + 256 + 32 * (i // 2 + 1))
                    if i == 2:
                        t.wait_ge(s_v1.h, V1_AP + NEV)
                        t.wait_ge(s_g1.h, G1_AP + 4 * NOD)
                    for hf in range(2):
                        hi = 2 * i + hf
                        if hi >= 2:
                            t.wait_ge(s_a1.h, A1_4 + 2 * (hi - 1))
                        for a in range(DT):
                            if i <= 1 and hf == 0:
                                apply_done_wait(t, a)
                            for nb in range(2):
                                t.matmul(hp[hf][:, nb * 512:(nb + 1) * 512],
                                         w1s[2 * (i % 2)][:, a, :],
                                         y1[:, a, hf * 1024 + nb * 512:
                                            hf * 1024 + (nb + 1) * 512],
                                         start=(a == 0), stop=(a == DT - 1))
                        for a in range(DT):
                            for nb in range(2):
                                mm = t.matmul(hp[2 + hf][:, nb * 512:(nb + 1) * 512],
                                              w1s[2 * (i % 2) + 1][:, a, :],
                                              y1[:, a, hf * 1024 + nb * 512:
                                                 hf * 1024 + (nb + 1) * 512],
                                              start=(a == 0), stop=(a == DT - 1))
                        mm.then_inc(s_t1.h, 1)
                for o in range(DT):
                    if o == 0:
                        t.wait_ge(s_a1.h, A1_5)
                    t.wait_ge(SG[o % 2].h,
                              GB3[o % 2] + 256 + 512 + 16 * (o // 2 + 1))
                    if o >= 2:
                        t.wait_ge(s_a1.h, A1_5 + o - 1)
                    for a in range(DT):
                        if o == 0 and a % 4 == 0:
                            gq = a // 4
                            if gq % 2 == 0:
                                t.wait_ge(SMs.h, SM5 + 16 * (gq // 2 + 1))
                            else:
                                t.wait_ge(SMg.h, SMG5 + 16 * (gq // 2 + 1))
                        if o == 1 and a == 0:
                            t.wait_ge(SMs.h, SM5 + 64)
                            t.wait_ge(SMg.h, SMG5 + 64)
                        for nb in range(4):
                            mm = t.matmul(wp5[o % 2][:, nb * 512:(nb + 1) * 512],
                                          w3s[o % 2][:, a, :],
                                          x4r[:, a, nb * 512:(nb + 1) * 512],
                                          start=(a == 0), stop=(a == DT - 1))
                    mm.then_inc(s_t1.h, 1)
                t.wait_ge(s_a1.h, A1_5 + DT)
                t.wait_ge(s_g1.h, G2_A)
                t.wait_ge(s_v3.h, V3_F2)
                for j in range(4):
                    t.matmul(wp5[0][:, j * 512:(j + 1) * 512], ones_f32[:],
                             A2b[:, j * 512:(j + 1) * 512], start=True, stop=True)
                for j in range(4):
                    mm = t.matmul(wp5[1][:, j * 512:(j + 1) * 512], ones_f32[:],
                                  SQ2b[:, j * 512:(j + 1) * 512],
                                  start=True, stop=True)
                mm.then_inc(s_t1.h, 1)

            @blk.scalar
            def _(sc):
                sc.dma_start(xres[:], xTown[0:128, :]).then_inc(SS[0].h, 16)
                for o in range(DT):
                    sc.wait_ge(s_t1.h, T1_3 + o + 1)
                    if o >= 2:
                        sc.wait_ge(SO[o % 2].h, OB3[o % 2] + 16 * (o // 2))
                        sc.wait_ge(s_g1.h, G1_3 + o - 1)
                        sc.wait_ge(s_v3.h, V3_3 + 5 * (o - 2) + 4)
                    sc.activation(P1[o % 2][:], wp[o % 2][:], AF.Identity,
                                  bias=bo[:, o:o + 1]).then_inc(s_a1.h, 1)
                    if o + 1 < DT:
                        sc.wait_ge(s_v3.h, V3_3 + 5 * o + 1)
                        sc.dma_start(xres[:],
                                     xTown[(o + 1) * 128:(o + 2) * 128, :]
                                     ).then_inc(SS[0].h, 16)
                sc.wait_ge(s_v3.h, V3_F0 + 6)
                sc.activation(SQb[:], RS1[:], AF.Sqrt).then_inc(s_a1.h, 1)
                for i in range(DT):
                    for hf in range(2):
                        hi = 2 * i + hf
                        sc.wait_ge(s_t1.h, T1_4 + hi + 1)
                        if hi >= 1:
                            sc.wait_ge(s_v1.h, V1_4 + hi)
                        sc.activation(sl[:], hp[hf][:], AF.Silu,
                                      bias=b12[:, i:i + 1]).then_inc(s_a1.h, 1)
                        sc.activation(h2s[:], hp[2 + hf][:], AF.Identity,
                                      bias=b12[:, 32 + i:33 + i]).then_inc(s_a1.h, 1)
                sc.wait_ge(s_v1.h, V1_L2)
                sc.wait_ge(SO[0].h, OB4[0] + 256)
                sc.wait_ge(SO[1].h, OB4[1] + 256)
                for gq in range(1, 8, 2):
                    sc.dma_start(
                        x4r[:, 4 * gq:4 * (gq + 1), :],
                        x4sp[4 * gq * 128:4 * (gq + 1) * 128, :].rearrange(
                            "(a p) u -> p a u", p=128)
                    ).then_inc(SMg.h, 16)
                sc.dma_start(pr32b[:], p1sp[0:128, :]).then_inc(SS[0].h, 16)
                for o in range(DT):
                    sc.wait_ge(s_t1.h, T1_5 + o + 1)
                    if o >= 2:
                        sc.wait_ge(SO[o % 2].h, OB5[o % 2] + 16 * (o // 2))
                        sc.wait_ge(s_g1.h, G1_5 + o - 1)
                        sc.wait_ge(s_v3.h, V3_5 + 10 * (o - 2) + 9)
                    sc.activation(P2[o % 2][:], wp5[o % 2][:], AF.Identity,
                                  bias=b3[:, o:o + 1]).then_inc(s_a1.h, 1)
                    if o + 1 < DT:
                        sc.wait_ge(s_v3.h, V3_5 + 10 * o + 6)
                        sc.dma_start(pr32b[:],
                                     p1sp[(o + 1) * 128:(o + 2) * 128, :]
                                     ).then_inc(SS[0].h, 16)
                sc.wait_ge(s_v3.h, V3_F2 + 6)
                sc.activation(SQ2b[:], RS2[:], AF.Sqrt).then_inc(s_a1.h, 1)
                for a in range(1, DT, 2):
                    k = a // 2
                    sc.wait_ge(s_g1.h, G1_L2 + 2 * (k + 1))
                    sc.dma_start(outT[a * 128:(a + 1) * 128, :],
                                 outst_g[k % 2][:]).then_inc(SO[1].h, 16)
                sc.wait_ge(SO[1].h, OB_F[1] + 16 * NOD)

            @blk.gpsimd
            def _(g):
                for o in range(2):
                    g.dma_start(
                        wos[o][:],
                        woT[:, o * 128:(o + 1) * 128].rearrange(
                            "(a p) o -> p a o", p=128)
                    ).then_inc(SG[o].h, 16)
                for o in range(DT):
                    if o + 2 < DT:
                        g.wait_ge(s_t1.h, T1_3 + o + 1)
                        g.dma_start(
                            wos[o % 2][:],
                            woT[:, (o + 2) * 128:(o + 3) * 128].rearrange(
                                "(a p) o -> p a o", p=128)
                        ).then_inc(SG[o % 2].h, 16)
                    g.wait_ge(s_v3.h, V3_3 + 5 * o + 1)
                    if o == 0:
                        g.tensor_copy(A1b[:], P1[0][:]).then_inc(s_g1.h, 1)
                    else:
                        g.tensor_add(A1b[:], A1b[:], P1[o % 2][:]).then_inc(s_g1.h, 1)
                    g.wait_ge(s_g1.h, G1_3 + o + 1)
                g.wait_ge(s_t1.h, T1_S)
                for i in range(2):
                    for k, col in ((0, i), (1, 32 + i)):
                        g.dma_start(
                            w1s[2 * (i % 2) + k][:],
                            w12T[:, col * 128:(col + 1) * 128].rearrange(
                                "(a p) o -> p a o", p=128)
                        ).then_inc(SG[i % 2].h, 16)
                for a in range(1, DT, 2):
                    k = a // 2
                    g.wait_ge(SQR[a % 2].h, QR3[a % 2] + 16 * (k + 1))
                    if k == 0:
                        g.wait_ge(s_v3.h, V3_FIN)
                    else:
                        g.wait_ge(s_g1.h, G1_AP + 4 * k)
                    for h in range(2):
                        g.tensor_mul(t_g[:], pr32[a % 2][:, h * HT:(h + 1) * HT],
                                     RS1[:, h * HT:(h + 1) * HT]).then_inc(s_g1.h, 1)
                        g.wait_ge(s_g1.h, G1_AP + 4 * k + 2 * h + 1)
                        g.tensor_sub(y1[:, a, h * HT:(h + 1) * HT], t_g[:],
                                     MRS1[:, h * HT:(h + 1) * HT]
                                     ).then_inc(s_g1.h, 1)
                        g.wait_ge(s_g1.h, G1_AP + 4 * k + 2 * h + 2)
                for i in range(2, DT):
                    g.wait_ge(s_t1.h, T1_4 + 2 * (i - 1))
                    for k, col in ((0, i), (1, 32 + i)):
                        g.dma_start(
                            w1s[2 * (i % 2) + k][:],
                            w12T[:, col * 128:(col + 1) * 128].rearrange(
                                "(a p) o -> p a o", p=128)
                        ).then_inc(SG[i % 2].h, 16)
                g.wait_ge(s_t1.h, T1_4 + 2 * DT)
                for gq in range(0, 8, 2):
                    g.wait_ge(SO[0].h, OB4[0] + 16 * (2 * gq + 2))
                    g.wait_ge(SO[1].h, OB4[1] + 16 * (2 * gq + 2))
                    g.dma_start(
                        x4r[:, 4 * gq:4 * (gq + 1), :],
                        x4sp[4 * gq * 128:4 * (gq + 1) * 128, :].rearrange(
                            "(a p) u -> p a u", p=128)
                    ).then_inc(SMs.h, 16)
                for o in range(2):
                    g.dma_start(
                        w3s[o][:],
                        w3T[:, o * 128:(o + 1) * 128].rearrange(
                            "(a p) o -> p a o", p=128)
                    ).then_inc(SG[o].h, 16)
                for o in range(DT):
                    if o + 2 < DT:
                        g.wait_ge(s_t1.h, T1_5 + o + 1)
                        g.dma_start(
                            w3s[o % 2][:],
                            w3T[:, (o + 2) * 128:(o + 3) * 128].rearrange(
                                "(a p) o -> p a o", p=128)
                        ).then_inc(SG[o % 2].h, 16)
                    g.wait_ge(s_v3.h, V3_5 + 10 * o + 8)
                    if o == 0:
                        g.tensor_copy(A2b[:], P2[0][:]).then_inc(s_g1.h, 1)
                    else:
                        g.tensor_add(A2b[:], A2b[:], P2[o % 2][:]).then_inc(s_g1.h, 1)
                    g.wait_ge(s_g1.h, G1_5 + o + 1)
                for a in range(1, DT, 2):
                    k = a // 2
                    g.wait_ge(SQR[a % 3].h, QR3b[a % 3] + 16 * (a // 3 + 1))
                    if k == 0:
                        g.wait_ge(s_v3.h, V3_FIN2)
                    else:
                        g.wait_ge(s_g1.h, G1_L2 + 2 * k)
                    if a >= 5:
                        g.wait_ge(SO[1].h, OB_F[1] + 16 * (k - 1))
                    g.tensor_mul(t_g2[:], pr32c[a % 3][:], RS2[:]).then_inc(s_g1.h, 1)
                    g.wait_ge(s_g1.h, G1_L2 + 2 * k + 1)
                    g.tensor_sub(outst_g[k % 2][:], t_g2[:], MRS2[:]
                                 ).then_inc(s_g1.h, 1)

            @blk.vector
            def _(v):
                def fence(instr):
                    instr.then_inc(s_v3.h, 1)
                    s_v3.n += 1
                    v.wait_ge(s_v3.h, s_v3.n)

                for o in range(DT):
                    v.wait_ge(s_a1.h, A1_3 + o + 1)
                    v.wait_ge(SS[0].h, SB3 + 16 * (o + 1))
                    fence(v.tensor_add(P1[o % 2][:], P1[o % 2][:], xres[:]))
                    for h in range(2):
                        fence(v.tensor_mul(sqs[:], P1[o % 2][:, h * HT:(h + 1) * HT],
                                           P1[o % 2][:, h * HT:(h + 1) * HT]))
                        if o == 0:
                            fence(v.tensor_copy(SQb[:, h * HT:(h + 1) * HT], sqs[:]))
                        else:
                            fence(v.tensor_add(SQb[:, h * HT:(h + 1) * HT],
                                               SQb[:, h * HT:(h + 1) * HT], sqs[:]))
                assert s_v3.n == V3_F0
                v.wait_ge(s_t1.h, T1_S + 1)
                v.wait_ge(SO[0].h, OB3[0] + 256)
                v.wait_ge(SO[1].h, OB3[1] + 256)
                v.wait_ge(s_g1.h, G1_A)
                fence(v.tensor_scalar_mul(A1b[:], wp[0][:], 1.0 / D))
                fence(v.tensor_scalar_mul(SQb[:], wp[1][:], 1.0 / D))
                fence(v.tensor_mul(RS1[:], A1b[:], A1b[:]))
                fence(v.tensor_sub(SQb[:], SQb[:], RS1[:]))
                fence(v.tensor_scalar_add(SQb[:], SQb[:], EPS))
                fence(v.reciprocal(RS1[:], SQb[:]))
                v.wait_ge(s_a1.h, A1_SQT)
                fence(v.tensor_copy(RS1[:], SQb[:]))
                fence(v.tensor_mul(MRS1[:], A1b[:], RS1[:]))
                assert s_v3.n == V3_FIN
                for a in range(0, DT, 2):
                    k = a // 2
                    v.wait_ge(SQR[a % 2].h, QR3[a % 2] + 16 * (k + 1))
                    if k >= 1:
                        v.wait_ge(s_v1.h, V1_AP + k)
                    for h in range(2):
                        fence(v.tensor_mul(t_v[:], pr32[a % 2][:, h * HT:(h + 1) * HT],
                                           RS1[:, h * HT:(h + 1) * HT]))
                        if h == 0:
                            fence(v.tensor_sub(y1[:, a, 0:HT], t_v[:],
                                               MRS1[:, 0:HT]))
                        else:
                            v.tensor_sub(y1[:, a, HT:TOWN], t_v[:],
                                         MRS1[:, HT:TOWN]).then_inc(s_v1.h, 1)
                assert s_v3.n == V3_AP
                for i in range(DT):
                    for hf in range(2):
                        hi = 2 * i + hf
                        v.wait_ge(s_a1.h, A1_4 + 2 * (hi + 1))
                        if hf == 0 and i >= 2:
                            v.wait_ge(SO[i % 2].h, OB4[i % 2] + 16 * (i // 2))
                        v.tensor_mul(x4st[i % 2][:, hf * 1024:(hf + 1) * 1024],
                                     sl[:], h2s[:]).then_inc(s_v1.h, 1)
                for o in range(DT):
                    v.wait_ge(SS[0].h, SB5 + 16 * (o + 1))
                    v.wait_ge(s_a1.h, A1_5 + o + 1)
                    for h in range(2):
                        hs = slice(h * HT, (h + 1) * HT)
                        fence(v.tensor_mul(sq2s[:], pr32b[:, hs], RS1[:, hs]))
                        fence(v.tensor_sub(sq2s[:], sq2s[:], MRS1[:, hs]))
                        fence(v.tensor_add(P2[o % 2][:, hs], P2[o % 2][:, hs],
                                           sq2s[:]))
                        fence(v.tensor_mul(sq2s[:], P2[o % 2][:, hs],
                                           P2[o % 2][:, hs]))
                        if o == 0:
                            fence(v.tensor_copy(SQ2b[:, hs], sq2s[:]))
                        else:
                            fence(v.tensor_add(SQ2b[:, hs], SQ2b[:, hs], sq2s[:]))
                assert s_v3.n == V3_F2
                v.wait_ge(s_t1.h, T1_5 + DT + 1)
                v.wait_ge(SO[0].h, OB5[0] + 256)
                v.wait_ge(SO[1].h, OB5[1] + 256)
                v.wait_ge(s_g1.h, G2_A)
                fence(v.tensor_scalar_mul(A2b[:], wp5[0][:], 1.0 / D))
                fence(v.tensor_scalar_mul(SQ2b[:], wp5[1][:], 1.0 / D))
                fence(v.tensor_mul(RS2[:], A2b[:], A2b[:]))
                fence(v.tensor_sub(SQ2b[:], SQ2b[:], RS2[:]))
                fence(v.tensor_scalar_add(SQ2b[:], SQ2b[:], EPS))
                fence(v.reciprocal(RS2[:], SQ2b[:]))
                v.wait_ge(s_a1.h, A2_SQT)
                fence(v.tensor_copy(RS2[:], SQ2b[:]))
                fence(v.tensor_mul(MRS2[:], A2b[:], RS2[:]))
                assert s_v3.n == V3_FIN2
                for a in range(0, DT, 2):
                    k = a // 2
                    v.wait_ge(SQR[a % 3].h, QR3b[a % 3] + 16 * (a // 3 + 1))
                    if k >= 1:
                        v.wait_ge(s_v1.h, V1_L2 + k)
                    if k >= 2:
                        v.wait_ge(SO[0].h, OB_F[0] + 16 * (k - 1))
                    fence(v.tensor_mul(t_v2[:], pr32c[a % 3][:], RS2[:]))
                    v.tensor_sub(outst_v[k % 2][:], t_v2[:], MRS2[:]
                                 ).then_inc(s_v1.h, 1)

            @blk.sync
            def _(sy):
                for o in range(DT):
                    sy.wait_ge(s_v3.h, V3_3 + 5 * o + 1)
                    sy.dma_start(p1sp[o * 128:(o + 1) * 128, :], P1[o % 2][:]
                                 ).then_inc(SO[o % 2].h, 16)
                for a in range(DT):
                    if a < 2:
                        sy.wait_ge(s_v3.h, V3_FIN)
                    else:
                        apply_done_wait(sy, a - 2)
                    sy.dma_start(pr32[a % 2][:], p1sp[a * 128:(a + 1) * 128, :]
                                 ).then_inc(SQR[a % 2].h, 16)
                for i in range(DT):
                    sy.wait_ge(s_v1.h, V1_4 + 2 * (i + 1))
                    sy.dma_start(x4sp[i * 128:(i + 1) * 128, :], x4st[i % 2][:]
                                 ).then_inc(SO[i % 2].h, 16)
                for o in range(DT):
                    sy.wait_ge(s_v3.h, V3_5 + 10 * o + 8)
                    sy.dma_start(p2sp[o * 128:(o + 1) * 128, :], P2[o % 2][:]
                                 ).then_inc(SO[o % 2].h, 16)
                for a in range(DT):
                    if a < 3:
                        sy.wait_ge(s_v3.h, V3_FIN2)
                    else:
                        am = a - 3
                        if am % 2 == 0:
                            sy.wait_ge(s_v1.h, V1_L2 + am // 2 + 1)
                        else:
                            sy.wait_ge(s_g1.h, G1_L2 + 2 * (am // 2 + 1))
                    sy.dma_start(pr32c[a % 3][:], p2sp[a * 128:(a + 1) * 128, :]
                                 ).then_inc(SQR[a % 3].h, 16)
                    if a >= 3 and (a - 3) % 2 == 0:
                        k = (a - 3) // 2
                        sy.dma_start(outT[(a - 3) * 128:(a - 2) * 128, :],
                                     outst_v[k % 2][:]).then_inc(SO[0].h, 16)
                sy.wait_ge(s_v1.h, V1_L2 + NEV)
                sy.dma_start(outT[30 * 128:31 * 128, :],
                             outst_v[15 % 2][:]).then_inc(SO[0].h, 16)
                sy.wait_ge(SO[0].h, OB_F[0] + 16 * NEV)

        SG[0].n = GB3[0] + 256 + 512 + 256
        SG[1].n = GB3[1] + 256 + 512 + 256
        SS[0].n = SB3 + 512 + 512
        SO[0].n = OB_F[0] + 16 * NEV
        SO[1].n = OB_F[1] + 16 * NOD
        SMs.n = SM5 + 64
        SMg.n = SMG5 + 64
        SQR[0].n = QR3[0] + 256 + 16 * 11
        SQR[1].n = QR3[1] + 256 + 16 * 11
        SQR[2].n = QR3[2] + 16 * 10
        s_t1.n = T1_5 + DT + 1
        s_a1.n = A2_SQT
        s_g1.n = G1_L2 + 2 * NOD
        s_v1.n = V1_L2 + NEV
        assert s_v3.n == V3_FIN2 + NEV

    return nc


def _make_masks(parity):
    masks = np.zeros((NCH, 128, MW * CH), dtype=np.float32)
    q = np.arange(CH)[None, :]
    k = np.arange(128)[:, None]
    for c in range(NCH):
        q0 = Q0S[parity][c]
        for w in range(MW):
            kt = EXTS[parity][c] - MW + w
            keep = (q0 + q) >= (kt * 128 + k)
            masks[c, :, w * CH:(w + 1) * CH] = np.where(keep, 0.0, -1.0e5)
    return masks


def kernel(L0_attn_x, L0_attn_wq_w, L0_attn_wq_b, L0_attn_wk_w, L0_attn_wk_b,
           L0_attn_wv_w, L0_attn_wv_b, L0_attn_wo_w, L0_attn_wo_b,
           L0_ffn_w12, L0_ffn_b12, L0_ffn_w3, L0_ffn_b3):
    x = np.asarray(L0_attn_x, dtype=np.float32)

    c = np.ascontiguousarray
    wqT = c(np.asarray(L0_attn_wq_w, np.float32).T)
    wkvT = c(np.concatenate([np.asarray(L0_attn_wk_w, np.float32).T,
                             np.asarray(L0_attn_wv_w, np.float32).T], axis=1))
    woT = c(np.asarray(L0_attn_wo_w, np.float32).T)
    w12T = c(np.asarray(L0_ffn_w12, np.float32).T)
    w3T = c(np.asarray(L0_ffn_w3, np.float32).T)
    bq = c(np.asarray(L0_attn_wq_b, np.float32).reshape(32, 128).T)
    bo = c(np.asarray(L0_attn_wo_b, np.float32).reshape(32, 128).T)
    b12 = c(np.asarray(L0_ffn_b12, np.float32).reshape(64, 128).T)
    b3 = c(np.asarray(L0_ffn_b3, np.float32).reshape(32, 128).T)
    bkv = c(np.stack([np.asarray(L0_attn_wk_b, np.float32),
                      np.asarray(L0_attn_wv_b, np.float32)], axis=1))
    ident = np.eye(128, dtype=np.float32)
    masks = [_make_masks(0), _make_masks(1)]

    nc = build_program()

    in_maps = []
    for core in range(8):
        b = core // 2
        parity = core % 2
        xTb = c(x[b].T)
        if parity == 0:
            xTown = c(np.concatenate([xTb[:, 0:1024], xTb[:, 3072:4096]], axis=1))
        else:
            xTown = c(xTb[:, 1024:3072])
        in_maps.append({
            "xT": xTb, "xTown": xTown, "wqT": wqT, "wkvT": wkvT, "woT": woT,
            "w12T": w12T, "w3T": w3T, "bqw": bq, "bow": bo, "b12w": b12, "b3w": b3,
            "bkvw": bkv, "masks": masks[parity], "identw": ident,
        })

    res = run_bass_kernel_spmd(nc, in_maps, core_ids=list(range(8)))
    kernel._last_results = res

    out = np.empty((BATCH, T, D), dtype=np.float32)
    for core in range(8):
        b = core // 2
        parity = core % 2
        oT = res.results[core]["outT"]
        if parity == 0:
            out[b, 0:1024, :] = oT[:, 0:1024].T
            out[b, 3072:4096, :] = oT[:, 1024:2048].T
        else:
            out[b, 1024:3072, :] = oT.T
    return out
